# version 17
# baseline (speedup 1.0000x reference)
import numpy as np
import ml_dtypes
from contextlib import ExitStack

import concourse.bass as bass
import concourse.mybir as mybir
from concourse.bass_utils import run_bass_kernel_spmd

F32 = mybir.dt.float32
BF16 = mybir.dt.bfloat16
I32 = mybir.dt.int32
AX = mybir.AxisListType
ALU = mybir.AluOpType
AF = mybir.ActivationFunctionType
IOA = bass.IndirectOffsetOnAxis
bfdt = ml_dtypes.bfloat16

NCORES = 8
B, N, D, H, HD = 16, 2048, 1024, 16, 64
E, F = 8, 4096
SEQ_PER_CORE = B // NCORES
TC = SEQ_PER_CORE * N
C = 1024
NBLK = TC // 128
OFFSETS = sorted(set(range(0, 33)) | {48, 64, 96, 128, 192, 256, 384, 512, 768, 1024, 1536})
DSET = [0, 1, 2, 3, 4, 6, 8, 12]
WLEN = 1792
SCALE = HD ** -0.5
EXP_SHIFT = -3.0


class Ctr:
    def __init__(self, sem):
        self.sem = sem
        self.v = 0

    def inc(self, n=1):
        self.v += n
        return self.v


def build_program(debug=False):
    nc = bass.Bass("TRN2", target_bir_lowering=False)

    x_in = nc.dram_tensor("x", [TC, D], F32, kind="ExternalInput")
    Wqkv = nc.dram_tensor("Wqkv", [D, 3 * D], F32, kind="ExternalInput")
    Wgate = nc.dram_tensor("Wgate", [D, D], F32, kind="ExternalInput")
    Wout = nc.dram_tensor("Wout", [D, D], F32, kind="ExternalInput")
    sww = nc.dram_tensor("sww", [D, E], F32, kind="ExternalInput")
    w1 = nc.dram_tensor("w1", [E, D, F], F32, kind="ExternalInput")
    w2 = nc.dram_tensor("w2", [E, F, D], F32, kind="ExternalInput")
    bqkv_cols = nc.dram_tensor("bqkv_cols", [128, 16], F32, kind="ExternalInput")
    bv_row = nc.dram_tensor("bv_row", [1, D], F32, kind="ExternalInput")
    bg_row = nc.dram_tensor("bg_row", [1, D], F32, kind="ExternalInput")
    bo_row = nc.dram_tensor("bo_row", [1, D], F32, kind="ExternalInput")
    swb_col = nc.dram_tensor("swb_col", [8, 1], F32, kind="ExternalInput")
    b1_cols = nc.dram_tensor("b1_cols", [128, 256], F32, kind="ExternalInput")
    b2_rows = nc.dram_tensor("b2_rows", [E, D], F32, kind="ExternalInput")
    pbT = nc.dram_tensor("pbT", [16, 44], F32, kind="ExternalInput")
    ident_bf = nc.dram_tensor("ident_bf", [128, 128], BF16, kind="ExternalInput")
    i8c = nc.dram_tensor("i8c", [8, 8], F32, kind="ExternalInput")
    ones8 = nc.dram_tensor("ones8", [8, 1], F32, kind="ExternalInput")
    onescol = nc.dram_tensor("onescol", [1, 128], F32, kind="ExternalInput")
    iota_tm_c = nc.dram_tensor("iota_tm_c", [128, 32], I32, kind="ExternalInput")
    idxM = nc.dram_tensor("idxM", [128, 128], I32, kind="ExternalInput")
    e8col = nc.dram_tensor("e8col", [8, 1], F32, kind="ExternalInput")

    out_d = nc.dram_tensor("out", [TC, D], F32, kind="ExternalOutput")
    if debug:
        dbg = {}
        for nm, shp, dt in [("d_xn", [TC, D], BF16), ("d_k", [128, 4096], BF16),
                            ("d_v65", [128, 2080], BF16), ("d_q", [4, 8, 128, 128], BF16),
                            ("d_gate", [TC, D], BF16), ("d_logits", [8, TC], F32),
                            ("d_P1", [128, 2048], BF16), ("d_P2", [128, 2048], BF16),
                            ("d_flat", [128, 1024], BF16), ("d_M", [128, 2048], BF16),
                            ("d_mask", [8, TC], F32), ("d_pm", [128, 32], F32),
                            ("d_dest", [128, 32], I32), ("d_g", [E * C, 1], I32),
                            ("d_yffn", [TC, D], BF16), ("d_partial", [TC, D], BF16),
                            ("d_xg", [128, 8192], BF16), ("d_stat", [128, 10], F32)]:
            dbg[nm] = nc.dram_tensor(nm, shp, dt, kind="ExternalOutput")

    xn_tm_d = nc.dram_tensor("xn_tm_d", [TC, D], BF16, kind="Internal")
    q_d = nc.dram_tensor("q_d", [SEQ_PER_CORE, 16, 8, 128, 128], BF16, kind="Internal")
    gate_d = nc.dram_tensor("gate_d", [TC, D], BF16, kind="Internal")
    partial_d = nc.dram_tensor("partial_d", [TC, D], BF16, kind="Internal")
    yffn_d = nc.dram_tensor("yffn_d", [TC, D], BF16, kind="Internal")
    w_toe_d = nc.dram_tensor("w_toe_d", [16, WLEN], BF16, kind="Internal")
    g_d = nc.dram_tensor("g_d", [E * C, 1], I32, kind="Internal")
    logits_d = nc.dram_tensor("logits_d", [8, TC], F32, kind="Internal")

    with ExitStack() as top:
        mksem = lambda n: Ctr(top.enter_context(nc.semaphore(n)))
        qd = mksem("qd")
        gd = mksem("gd")
        ts = mksem("ts")
        ss = mksem("ss")
        vs = mksem("vs")
        s_x = mksem("s_x")
        s_xnst = [mksem(f"s_xnst{i}") for i in range(2)]
        s_qst = [mksem(f"s_qst{i}") for i in range(2)]
        s_gst = mksem("s_gst")
        s_lgst = [mksem(f"s_lgst{i}") for i in range(2)]
        s_wt = mksem("s_wt")
        s_z = mksem("s_z")
        s_bld = [mksem(f"s_bld{i}") for i in range(2)]
        s_gx = mksem("s_gx")
        s_pst = mksem("s_pst")
        s_lgl = mksem("s_lgl")
        s_gsc = mksem("s_gsc")
        s_gld = mksem("s_gld")
        s_b2 = [mksem(f"s_b2_{i}") for i in range(2)]
        s_wq = mksem("s_wq")
        s_wv = mksem("s_wv")
        s_m = mksem("s_m")
        s_wg = mksem("s_wg")
        s_w1 = mksem("s_w1")
        s_w2 = mksem("s_w2")
        s_xg = mksem("s_xg")
        s_sc = [mksem(f"s_sc{i}") for i in range(2)]
        s_e = [mksem(f"s_e{i}") for i in range(4)]
        s_eo = [mksem(f"s_eo{i}") for i in range(4)]

        def vedge(inst):
            inst.then_inc(vs.sem, 1)
            nc.vector.wait_ge(vs.sem, vs.inc())

        def vflush(inst):
            vedge(inst)
            nc.vector.nop().then_inc(vs.sem, 1)
            return vs.inc()

        def sflush(inst):
            inst.then_inc(ss.sem, 1)
            nc.scalar.wait_ge(ss.sem, ss.inc())
            nc.scalar.nop().then_inc(ss.sem, 1)
            return ss.inc()

        sb = lambda n, s, d: top.enter_context(nc.sbuf_tensor(n, s, d))
        ident_sb = sb("ident_sb", [128, 128], BF16)
        i8_sb = sb("i8_sb", [8, 8], F32)
        ones8_sb = sb("ones8_sb", [8, 1], F32)
        onescol_sb = sb("onescol_sb", [1, 128], F32)
        iota_sb = sb("iota_sb", [128, 32], I32)
        idxM_sb = sb("idxM_sb", [128, 128], I32)
        e8_sb = sb("e8_sb", [8, 1], F32)
        bqkvc_sb = sb("bqkvc_sb", [128, 16], F32)
        swb_sb = sb("swb_sb", [8, 1], F32)
        b1c_sb = sb("b1c_sb", [128, 256], F32)
        pbT_sb = sb("pbT_sb", [16, 44], F32)
        epb_sb = sb("epb_sb", [16, 44], F32)
        pm_tm = sb("pm_tm", [128, 32], F32)

        for dst, src in [(ident_sb, ident_bf), (i8_sb, i8c), (ones8_sb, ones8),
                         (onescol_sb, onescol), (iota_sb, iota_tm_c), (idxM_sb, idxM),
                         (e8_sb, e8col), (bqkvc_sb, bqkv_cols), (swb_sb, swb_col),
                         (b1c_sb, b1_cols), (pbT_sb, pbT)]:
            nc.sync.dma_start(dst[:, :], src[:, :]).then_inc(qd.sem, 16)
            qd.inc(16)
        CONSTS = qd.v

        with ExitStack() as attn:
            asb = lambda n, s, d: attn.enter_context(nc.sbuf_tensor(n, s, d))
            Wqkv_bf = asb("Wqkv_bf", [128, 8 * 3072], BF16)
            WgWo_bf = asb("WgWo_bf", [128, 8 * 1024], BF16)
            sw_bf = asb("sw_bf", [128, 8 * 8], BF16)
            M_all = asb("M_all", [128, 16 * 8 * 128], BF16)
            k_fm = asb("k_fm", [128, 8 * 2048], BF16)
            v65 = asb("v65", [128, 16 * 1040], BF16)
            bv_sb = asb("bv_sb", [1, D], F32)
            bg_sb = asb("bg_sb", [1, D], F32)
            bo_sb = asb("bo_sb", [1, D], F32)
            for dst, src in [(bv_sb, bv_row), (bg_sb, bg_row), (bo_sb, bo_row)]:
                nc.sync.dma_start(dst[:, :], src[:, :]).then_inc(qd.sem, 16)
                qd.inc(16)
            CONSTS_ALL = qd.v

            for kc in range(8):
                nc.gpsimd.dma_start(WgWo_bf[:, kc * 1024:(kc + 1) * 1024],
                                    Wgate[kc * 128:(kc + 1) * 128, :]).then_inc(s_wg.sem, 16)
                s_wg.inc(16)
            WG_L0 = s_wg.v
            for kc in range(8):
                nc.gpsimd.dma_start(sw_bf[:, kc * 8:(kc + 1) * 8],
                                    sww[kc * 128:(kc + 1) * 128, :]).then_inc(s_wv.sem, 16)
                s_wv.inc(16)
            for kc in range(8):
                nc.gpsimd.dma_start(Wqkv_bf[:, kc * 3072 + 2048:(kc + 1) * 3072],
                                    Wqkv[kc * 128:(kc + 1) * 128, 2048:3072]).then_inc(s_wv.sem, 16)
                s_wv.inc(16)
            WV_L = s_wv.v
            for kc in range(8):
                nc.gpsimd.dma_start(Wqkv_bf[:, kc * 3072: kc * 3072 + 2048],
                                    Wqkv[kc * 128:(kc + 1) * 128, 0:2048]).then_inc(s_wq.sem, 16)
                s_wq.inc(16)
            WQKV_L = s_wq.v

            wtoe_ctx = ExitStack()
            w_toe = wtoe_ctx.enter_context(nc.sbuf_tensor("w_toe", [16, WLEN], BF16))
            nc.scalar.wait_ge(qd.sem, CONSTS)
            nc.scalar.activation(epb_sb[:, :], pbT_sb[:, :], AF.Exp).then_inc(ss.sem, 1)
            EPB = ss.inc()
            vedge(nc.vector.memset(w_toe[:, :], 0.0))
            nc.vector.wait_ge(ss.sem, EPB)
            last = None
            for i, dlt in enumerate(OFFSETS):
                last = nc.vector.tensor_copy(w_toe[:, 127 + dlt:128 + dlt], epb_sb[:, i:i + 1])
            WTOE = vflush(last)
            nc.sync.wait_ge(vs.sem, WTOE)
            nc.sync.dma_start(w_toe_d[:, :], w_toe[:, :]).then_inc(s_wt.sem, 16)
            WTOE_ST = s_wt.inc(16)
            with nc.sbuf_tensor("zb", [128, 1024], BF16) as zb, \
                 nc.sbuf_tensor("pois", [128, 64], I32) as pois:
                nc.vector.memset(zb[:, :], 0.0)
                ZME = vflush(nc.vector.memset(pois[:, :], 1 << 20))
                nc.sync.wait_ge(vs.sem, ZME)
                for b in range(NBLK):
                    nc.sync.dma_start(yffn_d[b * 128:(b + 1) * 128, :], zb[:, :]).then_inc(s_z.sem, 16)
                    s_z.inc(16)
                nc.sync.dma_start(g_d[:, :].rearrange("(p c) o -> p (c o)", p=128),
                                  pois[:, :]).then_inc(s_z.sem, 16)
                ZD = s_z.inc(16)
                nc.sync.wait_ge(s_z.sem, ZD)
                nc.all_engine_barrier()
            wtoe_ctx.close()
            nc.gpsimd.wait_ge(s_wt.sem, WTOE_ST)
            for h in range(16):
                for di in range(8):
                    col = h * 8 + di
                    nc.gpsimd.indirect_dma_start(
                        M_all[:, col * 128:(col + 1) * 128], None,
                        w_toe_d[:, :], IOA(idxM_sb[:, col:col + 1], 1),
                    ).then_inc(s_m.sem, 16)
                    s_m.inc(16)
            M_L = s_m.v

            for s in range(SEQ_PER_CORE):
                ts_base = s * N
                with ExitStack() as pa:
                    psb = lambda n, sh, d: pa.enter_context(nc.sbuf_tensor(f"{n}_s{s}", sh, d))
                    pps = lambda n, sh, d: pa.enter_context(nc.psum_tensor(f"{n}_s{s}", sh, d))
                    xn_half = psb("xn_half", [128, 8 * 1024], BF16)
                    x_sb = [psb(f"x_sb{i}", [128, 1024], F32) for i in range(2)]
                    xn_tm = [psb(f"xn_tm{i}", [128, 1024], BF16) for i in range(2)]
                    sq_scr = psb("sq_scr", [128, 256], F32)
                    stat = psb("stat", [128, 10], F32)
                    lg_ev = [psb(f"lg_ev{i}", [8, 128], F32) for i in range(2)]
                    qk_ev = [psb(f"qk_ev{i}", [128, 512], BF16) for i in range(2)]
                    psT = pps("psT", [128, 1024], BF16)
                    psQK = [pps(f"psQK{i}", [128, 512], F32) for i in range(2)]
                    psV = pps("psV", [128, 1024], F32)
                    psG = pps("psG", [128, 1024], F32)
                    psL = pps("psL", [8, 128], F32)

                    if s == 0:
                        WG_L = WG_L0
                    else:
                        nc.gpsimd.wait_ge(ts.sem, ts.v)
                        for kc in range(8):
                            nc.gpsimd.dma_start(WgWo_bf[:, kc * 1024:(kc + 1) * 1024],
                                                Wgate[kc * 128:(kc + 1) * 128, :]).then_inc(s_wg.sem, 16)
                            s_wg.inc(16)
                        WG_L = s_wg.v

                    v65v = v65[:, :].rearrange("p (b h r) -> p (b h) r", h=16, r=65)
                    nc.vector.wait_ge(ts.sem, ts.v)
                    nc.vector.memset(v65v[:, :, 64:65], 1.0).then_inc(vs.sem, 1)
                    vs.inc()

                    xg = [[], []]
                    xng = [[], []]
                    last_evac = None
                    last_XN = None
                    if s == 0:
                        gt_guard = 0
                        lg_guard = [0, 0]
                        vg_state = {"vs": 0, "ss": 0}
                    nc.tensor.wait_ge(qd.sem, CONSTS_ALL)
                    nc.tensor.wait_ge(s_wv.sem, WV_L)
                    nc.tensor.wait_ge(s_wg.sem, WG_L)
                    ev_of = {}
                    xn_extra = [[], []]
                    xnst_val = [0, 0]

                    def emit_vg(half_, blk_, EV_):
                        nonlocal gt_guard
                        vb = half_ * 8 + blk_
                        g0v = ts_base + half_ * 1024 + blk_ * 128
                        nc.tensor.wait_ge(vs.sem, max(EV_, vg_state["vs"]))
                        if vg_state["ss"]:
                            nc.tensor.wait_ge(ss.sem, vg_state["ss"])
                        for kc in range(8):
                            lhs = xn_half[:, kc * 1024 + blk_ * 128: kc * 1024 + (blk_ + 1) * 128]
                            st = (kc == 0)
                            nc.tensor.matmul(psV[:, 0:512], lhs,
                                             Wqkv_bf[:, kc * 3072 + 2048: kc * 3072 + 2560],
                                             start=st, stop=False)
                            nc.tensor.matmul(psV[:, 512:1024], lhs,
                                             Wqkv_bf[:, kc * 3072 + 2560: kc * 3072 + 3072],
                                             start=st, stop=False)
                            nc.tensor.matmul(psG[:, 0:512], lhs,
                                             WgWo_bf[:, kc * 1024: kc * 1024 + 512],
                                             start=st, stop=False)
                            nc.tensor.matmul(psG[:, 512:1024], lhs,
                                             WgWo_bf[:, kc * 1024 + 512: kc * 1024 + 1024],
                                             start=st, stop=False)
                            nc.tensor.matmul(psL[:, :], sw_bf[:, kc * 8:(kc + 1) * 8], lhs,
                                             start=st, stop=(kc == 7))
                        nc.tensor.matmul(psV[:, 0:512], onescol_sb[:, :], bv_sb[:, 0:512],
                                         start=False, stop=False)
                        nc.tensor.matmul(psV[:, 512:1024], onescol_sb[:, :], bv_sb[:, 512:1024],
                                         start=False, stop=True)
                        nc.tensor.matmul(psG[:, 0:512], onescol_sb[:, :], bg_sb[:, 0:512],
                                         start=False, stop=False)
                        mm_ = nc.tensor.matmul(psG[:, 512:1024], onescol_sb[:, :], bg_sb[:, 512:1024],
                                               start=False, stop=True)
                        mm_.then_inc(ts.sem, 1)
                        VG = ts.inc()
                        nc.vector.wait_ge(ts.sem, VG)
                        vo = v65[:, vb * 1040:(vb + 1) * 1040].rearrange("p (h r) -> p h r", r=65)[:, :, 0:64]
                        vi = psV[:, :].rearrange("p (h r) -> p h r", r=64)
                        nc.vector.tensor_copy(vo, vi).then_inc(vs.sem, 1)
                        vg_state["vs"] = vs.inc()
                        nc.scalar.wait_ge(ts.sem, VG)
                        gt_ev = xn_tm[blk_ % 2]
                        if gt_guard:
                            nc.scalar.wait_ge(s_gst.sem, gt_guard)
                        nc.scalar.wait_ge(s_xnst[blk_ % 2].sem, xnst_val[blk_ % 2])
                        GT = sflush(nc.scalar.activation(gt_ev[:, :], psG[:, :], AF.Sigmoid))
                        lbi = blk_ % 2
                        if lg_guard[lbi]:
                            nc.scalar.wait_ge(s_lgst[lbi].sem, lg_guard[lbi])
                        LGE = sflush(nc.scalar.activation(lg_ev[lbi][:, :], psL[:, :], AF.Identity,
                                                          bias=swb_sb[:, 0:1]))
                        vg_state["ss"] = ss.v
                        nc.sync.wait_ge(ss.sem, GT)
                        nc.sync.dma_start(gate_d[g0v:g0v + 128, :], gt_ev[:, :]).then_inc(s_gst.sem, 16)
                        gt_guard = s_gst.inc(16)
                        xn_extra[blk_ % 2] = [(s_gst.sem, gt_guard), (ss.sem, GT)]
                        nc.sync.wait_ge(ss.sem, LGE)
                        nc.sync.dma_start(logits_d[:, g0v:g0v + 128], lg_ev[lbi][:, :]).then_inc(s_lgst[lbi].sem, 16)
                        lg_guard[lbi] = s_lgst[lbi].inc(16)
                    for half in range(2):
                        h0 = ts_base + half * 1024
                        for blk in range(8):
                            g0 = h0 + blk * 128
                            bi = blk % 2
                            for gsem, gval in xg[bi]:
                                nc.sync.wait_ge(gsem, gval)
                            nc.sync.dma_start(x_sb[bi][:, :], x_in[g0:g0 + 128, :]).then_inc(s_x.sem, 16)
                            XBLK = s_x.inc(16)
                            nc.vector.wait_ge(s_x.sem, XBLK)
                            nc.vector.reduce_sum(stat[:, 0:1], x_sb[bi][:, :], axis=AX.X)
                            nc.scalar.wait_ge(s_x.sem, XBLK)
                            if last_XN is not None:
                                nc.scalar.wait_ge(vs.sem, last_XN)
                            nc.scalar.activation(sq_scr[:, :], x_sb[bi][:, 0:256], AF.Square,
                                                 accum_out=stat[:, 1:2])
                            nc.scalar.activation(sq_scr[:, :], x_sb[bi][:, 256:512], AF.Square,
                                                 accum_out=stat[:, 2:3])
                            nc.scalar.activation(sq_scr[:, :], x_sb[bi][:, 512:768], AF.Square,
                                                 accum_out=stat[:, 0:1] if False else stat[:, 4:5])
                            nc.scalar.activation(sq_scr[:, :], x_sb[bi][:, 768:1024], AF.Square,
                                                 accum_out=stat[:, 5:6]).then_inc(ss.sem, 1)
                            SQ = ss.inc()
                            nc.vector.wait_ge(ss.sem, SQ)
                            nc.vector.tensor_tensor(stat[:, 6:7], stat[:, 1:2], stat[:, 2:3], ALU.add)
                            vedge(nc.vector.tensor_tensor(stat[:, 7:8], stat[:, 4:5], stat[:, 5:6], ALU.add))
                            nc.vector.tensor_scalar_mul(stat[:, 3:4], stat[:, 0:1], 1.0 / D)
                            vedge(nc.vector.tensor_tensor(stat[:, 1:2], stat[:, 6:7], stat[:, 7:8], ALU.add))
                            vedge(nc.vector.tensor_scalar_mul(stat[:, 2:3], stat[:, 1:2], 1.0 / D))
                            vedge(nc.vector.tensor_tensor(stat[:, 6:7], stat[:, 3:4], stat[:, 3:4], ALU.mult))
                            nc.vector.tensor_tensor(stat[:, 7:8], stat[:, 2:3], stat[:, 6:7],
                                                    ALU.subtract).then_inc(vs.sem, 1)
                            VAR = vs.inc()
                            nc.scalar.wait_ge(vs.sem, VAR)
                            nc.scalar.activation(stat[:, 8:9], stat[:, 7:8], AF.Sqrt,
                                                 bias=np.float32(1e-5)).then_inc(ss.sem, 1)
                            SD = ss.inc()
                            nc.vector.wait_ge(ss.sem, SD)
                            vedge(nc.vector.reciprocal(stat[:, 9:10], stat[:, 8:9]))
                            for gsem, gval in xng[bi] + xn_extra[bi]:
                                nc.vector.wait_ge(gsem, gval)
                            XN = vflush(nc.vector.tensor_scalar(xn_tm[bi][:, :], x_sb[bi][:, :],
                                                                 stat[:, 3:4], stat[:, 9:10],
                                                                 ALU.subtract, ALU.mult))
                            last_XN = XN
                            xg[bi] = [(vs.sem, XN)]
                            nc.sync.wait_ge(vs.sem, XN)
                            nc.sync.dma_start(xn_tm_d[g0:g0 + 128, :], xn_tm[bi][:, :]).then_inc(s_xnst[bi].sem, 16)
                            XNST = (s_xnst[bi].sem, s_xnst[bi].inc(16))
                            xnst_val[bi] = XNST[1]
                            nc.tensor.wait_ge(vs.sem, XN)
                            if last_evac is not None:
                                nc.tensor.wait_ge(vs.sem, last_evac)
                            for ch in range(8):
                                mm = nc.tensor.transpose(psT[:, ch * 128:(ch + 1) * 128],
                                                         xn_tm[bi][:, ch * 128:(ch + 1) * 128],
                                                         ident_sb[:, :])
                            mm.then_inc(ts.sem, 1)
                            TR = ts.inc()
                            nc.vector.wait_ge(ts.sem, TR)
                            out_ap = xn_half[:, :].rearrange("p (c t) -> p c t", c=8)[:, :, blk * 128:(blk + 1) * 128]
                            in_ap = psT[:, :].rearrange("p (c t) -> p c t", c=8)
                            nc.vector.tensor_copy(out_ap, in_ap).then_inc(vs.sem, 1)
                            EV = vs.inc()
                            last_evac = EV
                            xng[bi] = [XNST, (vs.sem, EV)]
                            if blk >= 1:
                                emit_vg(half, blk - 1, ev_of[blk - 1])
                            ev_of[blk] = EV
                        emit_vg(half, 7, ev_of[7])
                        XN_READY = vs.v

                        nc.tensor.wait_ge(s_wq.sem, WQKV_L)
                        nc.tensor.wait_ge(vs.sem, XN_READY)
                        if half == 0 and s == 0:
                            qk_guard = [0, 0]
                            qk_ev_done = [0, 0]
                        for oc in range(16):
                            for sub in range(2):
                                it = oc * 2 + sub
                                ei = it % 2
                                if qk_ev_done[ei]:
                                    nc.tensor.wait_ge(ss.sem, qk_ev_done[ei])
                                for kc in range(8):
                                    mm = nc.tensor.matmul(
                                        psQK[ei][:, :],
                                        Wqkv_bf[:, kc * 3072 + oc * 128: kc * 3072 + (oc + 1) * 128],
                                        xn_half[:, kc * 1024 + sub * 512: kc * 1024 + (sub + 1) * 512],
                                        start=(kc == 0), stop=(kc == 7))
                                mm.then_inc(ts.sem, 1)
                                MM = ts.inc()
                                nc.scalar.wait_ge(ts.sem, MM)
                                if oc < 8:
                                    if qk_guard[ei]:
                                        nc.scalar.wait_ge(s_qst[ei].sem, qk_guard[ei])
                                    QE = sflush(nc.scalar.activation(qk_ev[ei][:, :], psQK[ei][:, :], AF.Identity,
                                                                      bias=bqkvc_sb[:, oc:oc + 1]))
                                    nc.sync.wait_ge(ss.sem, QE)
                                    qb0 = half * 8 + sub * 4
                                    dst = q_d[s, qb0:qb0 + 4, oc, :, :].transpose([1, 0, 2])
                                    src = qk_ev[ei][:, :].rearrange("p (q t) -> p q t", q=4)
                                    nc.sync.dma_start(dst, src).then_inc(s_qst[ei].sem, 16)
                                    qk_guard[ei] = s_qst[ei].inc(16)
                                else:
                                    kk = oc - 8
                                    nc.scalar.activation(
                                        k_fm[:, kk * 2048 + half * 1024 + sub * 512:
                                             kk * 2048 + half * 1024 + (sub + 1) * 512],
                                        psQK[ei][:, :], AF.Identity,
                                        bias=bqkvc_sb[:, oc:oc + 1]).then_inc(ss.sem, 1)
                                    ss.inc()
                                qk_ev_done[ei] = ss.v

                    if debug and s == 0:
                        nc.sync.wait_ge(vs.sem, vs.v)
                        nc.sync.wait_ge(ss.sem, ss.v)
                        for c_ in [s_x, s_xnst[0], s_xnst[1], s_qst[0], s_qst[1],
                                   s_gst, s_lgst[0], s_lgst[1]]:
                            nc.sync.wait_ge(c_.sem, c_.v)
                        nc.sync.wait_ge(s_m.sem, s_m.v)
                        nc.sync.dma_start(dbg["d_stat"][:, :], stat[:, :]).then_inc(qd.sem, 16)
                        nc.sync.dma_start(dbg["d_k"][:, :], k_fm[:, 0:4096]).then_inc(qd.sem, 16)
                        nc.sync.dma_start(dbg["d_v65"][:, :], v65[:, 0:2080]).then_inc(qd.sem, 16)
                        nc.sync.dma_start(dbg["d_xn"][:, :], xn_tm_d[:, :]).then_inc(qd.sem, 16)
                        nc.sync.dma_start(dbg["d_q"][:, :, :, :], q_d[0, 0:4]).then_inc(qd.sem, 16)
                        nc.sync.dma_start(dbg["d_gate"][:, :], gate_d[:, :]).then_inc(qd.sem, 16)
                        nc.sync.dma_start(dbg["d_logits"][:, :], logits_d[:, :]).then_inc(qd.sem, 16)
                        nc.sync.dma_start(dbg["d_M"][:, :], M_all[:, 0:2048]).then_inc(qd.sem, 16)
                        qd.inc(8 * 16)
                        nc.sync.wait_ge(qd.sem, qd.v)
                    for c_ in [s_x, s_xnst[0], s_xnst[1], s_qst[0], s_qst[1],
                               s_gst, s_lgst[0], s_lgst[1]]:
                        nc.sync.wait_ge(c_.sem, c_.v)
                    nc.gpsimd.wait_ge(s_wg.sem, s_wg.v)
                    nc.all_engine_barrier()

                with ExitStack() as pb:
                    psb = lambda n, sh, d: pb.enter_context(nc.sbuf_tensor(f"{n}_s{s}", sh, d))
                    pps = lambda n, sh, d: pb.enter_context(nc.psum_tensor(f"{n}_s{s}", sh, d))
                    q_qb = [psb(f"q_qb{i}", [128, 1024], BF16) for i in range(2)]
                    gate_one = psb("gate_one", [128, 1024], BF16)
                    gate_qb = [gate_one, gate_one]
                    x_one = psb("x_one", [128, 1024], F32)
                    x_qb = [x_one, x_one]
                    P1 = [psb(f"P1_{i}", [128, 2048], BF16) for i in range(2)]
                    P2 = [psb(f"P2_{i}", [128, 2048], BF16) for i in range(2)]
                    flat_qb = psb("flat_qb", [128, 1024], BF16)
                    rec16 = psb("rec16", [128, 16], F32)
                    part_one = psb("part_one", [128, 1024], BF16)
                    part_ev = [part_one, part_one]
                    psSp = pps("psSp", [128, 2048], F32)
                    psS = [psSp[:, 0:1024], psSp[:, 1024:2048]]
                    psMulti = pps("psMulti", [128, 1536], F32)
                    psT2 = pps("psT2", [128, 1024], BF16)
                    pvcol = lambda u: (u // 7) * 512 + (u % 7) * 65

                    nc.gpsimd.wait_ge(ts.sem, ts.v)
                    for kc in range(8):
                        nc.gpsimd.dma_start(WgWo_bf[:, kc * 1024:(kc + 1) * 1024],
                                            Wout[kc * 128:(kc + 1) * 128, :]).then_inc(s_wg.sem, 16)
                        s_wg.inc(16)
                    WO_L = s_wg.v
                    nc.tensor.wait_ge(s_m.sem, M_L)

                    pf_guard = {}
                    gx_guard = []
                    if s == 0:
                        part_guard = [0, 0]
                    prev_PTE = 0
                    for qb in range(16):
                        bi = qb % 2
                        g0 = ts_base + qb * 128
                        for gsem, gval in pf_guard.get(bi, []):
                            nc.sync.wait_ge(gsem, gval)
                        nc.sync.dma_start(q_qb[bi][:, :],
                                          q_d[s, qb].transpose([1, 0, 2])).then_inc(s_bld[bi].sem, 16)
                        s_bld[bi].inc(16)
                        QLOAD = s_bld[bi].v
                        for gsem, gval in gx_guard:
                            nc.sync.wait_ge(gsem, gval)
                        nc.sync.dma_start(gate_qb[bi][:, :], gate_d[g0:g0 + 128, :]).then_inc(s_gx.sem, 16)
                        nc.sync.dma_start(x_qb[bi][:, :], x_in[g0:g0 + 128, :]).then_inc(s_gx.sem, 16)
                        GXLOAD = s_gx.inc(32)
                        nc.tensor.wait_ge(s_bld[bi].sem, QLOAD)
                        valid = [di for di, dd in enumerate(DSET) if qb - dd >= 0]
                        nv = len(valid)
                        exp_done = [0] * 8
                        msk_done = [0] * 8
                        pvd = [0] * 8
                        if qb == 0:
                            prev_pv_last = 0
                            if s == 0:
                                prev_msk_m1 = prev_msk_m2 = 0

                        def emit_pv(c):
                            nc.tensor.wait_ge(vs.sem, msk_done[c])
                            if c == 0:
                                nc.tensor.wait_ge(vs.sem, prev_PTE)
                            mm = None
                            for hh in range(2):
                                hg = 2 * c + hh
                                col = pvcol(hg)
                                for j, di in enumerate(valid):
                                    kb = qb - DSET[di]
                                    mm = nc.tensor.matmul(
                                        psMulti[:, col:col + 65],
                                        P2[c % 2][:, (hh * 8 + di) * 128: (hh * 8 + di + 1) * 128],
                                        v65[:, kb * 1040 + hg * 65: kb * 1040 + (hg + 1) * 65],
                                        start=(j == 0), stop=(j == len(valid) - 1))
                            mm.then_inc(ts.sem, 1)
                            return ts.inc()

                        for c in range(8):
                            if c >= 1:
                                nc.tensor.wait_ge(ss.sem, exp_done[c - 1])
                            elif qb > 0:
                                nc.tensor.wait_ge(ss.sem, ss.v)
                            mm = None
                            for di in valid:
                                kb = qb - DSET[di]
                                for hh in range(2):
                                    mm = nc.tensor.matmul(
                                        psS[hh][:, di * 128:(di + 1) * 128],
                                        k_fm[64 * hh:64 * hh + 64,
                                             c * 2048 + kb * 128: c * 2048 + (kb + 1) * 128],
                                        q_qb[bi][64 * hh:64 * hh + 64, c * 128:(c + 1) * 128],
                                        start=True, stop=True, tile_position=(64 * hh, 0))
                            mm.then_inc(ts.sem, 1)
                            SPD = ts.inc()
                            nc.scalar.wait_ge(ts.sem, SPD)
                            if c >= 2:
                                nc.scalar.wait_ge(vs.sem, msk_done[c - 2])
                            elif qb > 0:
                                nc.scalar.wait_ge(vs.sem, prev_msk_m1 if c == 1 else prev_msk_m2)
                            if nv == 8:
                                act = nc.scalar.activation(
                                    P1[c % 2][:, :], psSp[:, :],
                                    AF.Exp, bias=np.float32(EXP_SHIFT), scale=np.float32(SCALE))
                            else:
                                for hh in range(2):
                                    act = nc.scalar.activation(
                                        P1[c % 2][:, hh * 1024: hh * 1024 + nv * 128],
                                        psS[hh][:, 0: nv * 128],
                                        AF.Exp, bias=np.float32(EXP_SHIFT), scale=np.float32(SCALE))
                            act.then_inc(ss.sem, 1)
                            exp_done[c] = ss.inc()
                            nc.vector.wait_ge(ss.sem, exp_done[c])
                            if c >= 2:
                                nc.vector.wait_ge(ts.sem, pvd[c - 2])
                            elif qb > 0:
                                nc.vector.wait_ge(ts.sem, prev_pv_last)
                            tt = None
                            for hh in range(2):
                                hg = 2 * c + hh
                                tt = nc.vector.tensor_tensor(
                                    P2[c % 2][:, hh * 1024: hh * 1024 + nv * 128],
                                    P1[c % 2][:, hh * 1024: hh * 1024 + nv * 128],
                                    M_all[:, hg * 1024: hg * 1024 + nv * 128], ALU.mult)
                            tt.then_inc(vs.sem, 1)
                            msk_done[c] = vs.inc()
                            if debug and s == 0 and qb == 3 and c == 0:
                                nc.sync.wait_ge(vs.sem, msk_done[c])
                                nc.sync.dma_start(dbg["d_P1"][:, :], P1[c % 2][:, :]).then_inc(qd.sem, 16)
                                nc.sync.dma_start(dbg["d_P2"][:, :], P2[0][:, :]).then_inc(qd.sem, 16)
                                DBGP = qd.inc(32)
                                nc.scalar.wait_ge(qd.sem, DBGP)
                                nc.vector.wait_ge(qd.sem, DBGP)
                                nc.tensor.wait_ge(qd.sem, DBGP)
                            if c >= 1:
                                pvd[c - 1] = emit_pv(c - 1)
                        pvd[7] = emit_pv(7)
                        prev_pv_last = pvd[7]
                        prev_msk_m1 = msk_done[7]
                        prev_msk_m2 = msk_done[6]
                        nc.vector.wait_ge(ts.sem, pvd[7])
                        for g, ng in ((0, 7), (1, 7), (2, 2)):
                            vw = psMulti[:, g * 512: g * 512 + ng * 65].rearrange(
                                "p (u r) -> p u r", r=65)
                            inst = nc.vector.reciprocal(rec16[:, g * 7: g * 7 + ng], vw[:, :, 64:65])
                        vedge(inst)
                        tt = None
                        for u in range(16):
                            tt = nc.vector.tensor_scalar_mul(
                                flat_qb[:, u * 64:(u + 1) * 64],
                                psMulti[:, pvcol(u): pvcol(u) + 64],
                                rec16[:, u:u + 1])
                        tt.then_inc(vs.sem, 1)
                        vs.inc()
                        if debug and s == 0 and qb == 3:
                            nc.sync.wait_ge(vs.sem, vs.v)
                            nc.sync.dma_start(dbg["d_flat"][:, :], flat_qb[:, :]).then_inc(qd.sem, 16)
                            DBGF = qd.inc(16)
                            nc.vector.wait_ge(qd.sem, DBGF)
                        fg_sb = q_qb[bi]
                        fgT = gate_one
                        nc.vector.wait_ge(s_gx.sem, GXLOAD)
                        nc.vector.wait_ge(vs.sem, vs.v)
                        nc.vector.tensor_tensor(fg_sb[:, :], flat_qb[:, :], gate_qb[bi][:, :],
                                                ALU.mult).then_inc(vs.sem, 1)
                        FG = vs.inc()
                        nc.tensor.wait_ge(vs.sem, FG)
                        mm = None
                        for ch in range(8):
                            mm = nc.tensor.transpose(psT2[:, ch * 128:(ch + 1) * 128],
                                                     fg_sb[:, ch * 128:(ch + 1) * 128], ident_sb[:, :])
                        mm.then_inc(ts.sem, 1)
                        FGT = ts.inc()
                        nc.vector.wait_ge(ts.sem, FGT)
                        nc.vector.tensor_copy(fgT[:, :], psT2[:, :]).then_inc(vs.sem, 1)
                        FGE = vs.inc()
                        nc.tensor.wait_ge(vs.sem, FGE)
                        nc.tensor.wait_ge(s_wg.sem, WO_L)
                        for kc in range(8):
                            nc.tensor.matmul(psMulti[:, 0:512], fgT[:, kc * 128:(kc + 1) * 128],
                                             WgWo_bf[:, kc * 1024: kc * 1024 + 512],
                                             start=(kc == 0), stop=False)
                            nc.tensor.matmul(psMulti[:, 512:1024], fgT[:, kc * 128:(kc + 1) * 128],
                                             WgWo_bf[:, kc * 1024 + 512: kc * 1024 + 1024],
                                             start=(kc == 0), stop=False)
                        nc.tensor.matmul(psMulti[:, 0:512], onescol_sb[:, :], bo_sb[:, 0:512],
                                         start=False, stop=False)
                        mm = nc.tensor.matmul(psMulti[:, 512:1024], onescol_sb[:, :], bo_sb[:, 512:1024],
                                              start=False, stop=True)
                        mm.then_inc(ts.sem, 1)
                        WOD = ts.inc()
                        nc.vector.wait_ge(ts.sem, WOD)
                        if part_guard[bi]:
                            nc.vector.wait_ge(s_pst.sem, part_guard[bi])
                        PTE = vflush(nc.vector.tensor_tensor(part_ev[bi][:, :], psMulti[:, 0:1024],
                                                              x_qb[bi][:, :], ALU.add))
                        prev_PTE = PTE
                        nc.sync.wait_ge(vs.sem, PTE)
                        nc.sync.dma_start(partial_d[g0:g0 + 128, :], part_ev[bi][:, :]).then_inc(s_pst.sem, 16)
                        part_guard[0] = part_guard[1] = s_pst.inc(16)
                        pf_guard[bi] = [(ts.sem, ts.v), (vs.sem, vs.v)]
                        gx_guard = [(ts.sem, ts.v), (vs.sem, vs.v)]
                    for c_ in [s_bld[0], s_bld[1], s_gx, s_pst]:
                        nc.sync.wait_ge(c_.sem, c_.v)
                    nc.gpsimd.wait_ge(s_wg.sem, s_wg.v)
                    nc.all_engine_barrier()

        w1buf_ctx = ExitStack()
        w1buf = w1buf_ctx.enter_context(nc.sbuf_tensor("w1buf", [128, 8 * 4096], BF16))
        nc.gpsimd.dma_start(
            w1buf[:, :].rearrange("p (kc f) -> p kc f", kc=8),
            w1[0].rearrange("(kc p) f -> kc p f", p=128).transpose([1, 0, 2]),
        ).then_inc(s_w1.sem, 16)
        s_w1.inc(16)
        W1_0 = s_w1.v
        with ExitStack() as pc:
            psb = lambda n, sh, d: pc.enter_context(nc.sbuf_tensor(n, sh, d))
            pps = lambda n, sh, d: pc.enter_context(nc.psum_tensor(n, sh, d))
            logits_fm = psb("logits_fm", [8, TC], F32)
            lt_tm = psb("lt_tm", [128, 256], F32)
            e8x = psb("e8x", [128, 256], F32)
            e8y = psb("e8y", [128, 256], F32)
            m8 = psb("m8", [128, 32], F32)
            s8 = psb("s8", [128, 32], F32)
            mask_tm = psb("mask_tm", [128, 256], BF16)
            mask_fm = psb("mask_fm", [8, TC], F32)
            csA = psb("csA", [8, TC], F32)
            csB = psb("csB", [8, TC], F32)
            qf = psb("qf", [8, TC], F32)
            qi = psb("qi", [8, TC], I32)
            rf = psb("rf", [8, TC], F32)
            posb = psb("posb", [8, TC], F32)
            dest_i = psb("dest_i", [128, 32], I32)
            psL2 = pps("psL2", [128, 8], F32)
            psM = pps("psM", [8, 128], F32)
            psD2 = pps("psD2", [128, 32], F32)

            nc.sync.dma_start(logits_fm[:, :], logits_d[:, :]).then_inc(s_lgl.sem, 16)
            LGF = s_lgl.inc(16)
            nc.tensor.wait_ge(qd.sem, CONSTS_ALL)
            nc.tensor.wait_ge(s_lgl.sem, LGF)
            for b in range(NBLK):
                nc.tensor.matmul(psL2[:, :], logits_fm[:, b * 128:(b + 1) * 128],
                                 i8_sb[:, :], start=True, stop=True).then_inc(ts.sem, 1)
                LT = ts.inc()
                nc.vector.wait_ge(ts.sem, LT)
                nc.vector.tensor_copy(lt_tm[:, b * 8:(b + 1) * 8], psL2[:, :]).then_inc(vs.sem, 1)
                vs.inc()
                nc.tensor.wait_ge(vs.sem, vs.v)
            nc.vector.wait_ge(vs.sem, vs.v)
            vedge(nc.vector.reduce_max(m8[:, :], lt_tm[:, :].rearrange("p (b e) -> p b e", e=8), axis=AX.X))
            tt = None
            for b in range(NBLK):
                tt = nc.vector.tensor_scalar(e8x[:, b * 8:(b + 1) * 8], lt_tm[:, b * 8:(b + 1) * 8],
                                             m8[:, b:b + 1], None, ALU.subtract)
            tt.then_inc(vs.sem, 1)
            SUBD = vs.inc()
            nc.scalar.wait_ge(vs.sem, SUBD)
            nc.scalar.activation(e8y[:, :], e8x[:, :], AF.Exp).then_inc(ss.sem, 1)
            EXPD = ss.inc()
            nc.vector.wait_ge(ss.sem, EXPD)
            vedge(nc.vector.reduce_sum(s8[:, :], e8y[:, :].rearrange("p (b e) -> p b e", e=8), axis=AX.X))
            nc.vector.reciprocal(pm_tm[:, :], s8[:, :])
            tt = None
            for b in range(NBLK):
                tt = nc.vector.tensor_scalar(mask_tm[:, b * 8:(b + 1) * 8], lt_tm[:, b * 8:(b + 1) * 8],
                                             m8[:, b:b + 1], None, ALU.is_ge)
            tt.then_inc(vs.sem, 1)
            MTM = vs.inc()
            nc.tensor.wait_ge(vs.sem, MTM)
            for b in range(NBLK):
                nc.tensor.matmul(psM[:, :], mask_tm[:, b * 8:(b + 1) * 8], ident_sb[:, :],
                                 start=True, stop=True).then_inc(ts.sem, 1)
                MF = ts.inc()
                nc.vector.wait_ge(ts.sem, MF)
                nc.vector.tensor_copy(mask_fm[:, b * 128:(b + 1) * 128], psM[:, :]).then_inc(vs.sem, 1)
                vs.inc()
                nc.tensor.wait_ge(vs.sem, vs.v)
            nc.vector.wait_ge(vs.sem, vs.v)
            vedge(nc.vector.tensor_copy(csA[:, :], mask_fm[:, :]))
            src, dst = csA, csB
            sh = 1
            while sh < TC:
                nc.vector.tensor_tensor(dst[:, sh:], src[:, sh:], src[:, 0:TC - sh], ALU.add)
                vedge(nc.vector.tensor_copy(dst[:, 0:sh], src[:, 0:sh]))
                src, dst = dst, src
                sh *= 2
            csum = src
            assert csum is csA, "cumsum parity: result must land in csA"
            vbuf = csB
            vedge(nc.vector.tensor_scalar_add(vbuf[:, :], csum[:, :], -1.0))
            pos2 = csA
            vedge(nc.vector.tensor_scalar(qf[:, :], vbuf[:, :], 63.5, 1.0 / 128.0, ALU.subtract, ALU.mult))
            vedge(nc.vector.tensor_copy(qi[:, :], qf[:, :]))
            vedge(nc.vector.tensor_copy(qf[:, :], qi[:, :]))
            vedge(nc.vector.scalar_tensor_tensor(rf[:, :], qf[:, :], -128.0, vbuf[:, :], ALU.mult, ALU.add))
            vedge(nc.vector.scalar_tensor_tensor(posb[:, :], rf[:, :], 64.0, qf[:, :], ALU.mult, ALU.add))
            vedge(nc.vector.tensor_scalar_add(pos2[:, :], posb[:, :], e8_sb[:, 0:1]))
            vedge(nc.vector.tensor_single_scalar(rf[:, :], vbuf[:, :], C - 0.5, ALU.is_ge))
            vedge(nc.vector.scalar_tensor_tensor(posb[:, :], rf[:, :], 1.0e7, pos2[:, :], ALU.mult, ALU.add))
            nc.vector.tensor_tensor(pos2[:, :], posb[:, :], mask_fm[:, :],
                                    ALU.mult).then_inc(vs.sem, 1)
            POSD = vs.inc()
            nc.tensor.wait_ge(vs.sem, POSD)
            mm = None
            for b in range(NBLK):
                mm = nc.tensor.matmul(psD2[:, b:b + 1], pos2[:, b * 128:(b + 1) * 128],
                                      ones8_sb[:, :], start=True, stop=True)
            mm.then_inc(ts.sem, 1)
            DSTT = ts.inc()
            nc.vector.wait_ge(ts.sem, DSTT)
            DSTI = vflush(nc.vector.tensor_copy(dest_i[:, :], psD2[:, :]))
            nc.gpsimd.wait_ge(vs.sem, DSTI)
            with nc.gpsimd.register("bc_c") as bc_c:
                nc.gpsimd.reg_mov(bc_c, E * C - 1)
                for b in range(NBLK):
                    nc.gpsimd.indirect_dma_start(
                        g_d[:, :], IOA(dest_i[:, b:b + 1], 0),
                        iota_sb[:, b:b + 1], None,
                        bounds_check=bc_c, oob_is_err=False,
                    ).then_inc(s_gsc.sem, 16)
                    s_gsc.inc(16)
            if debug:
                nc.sync.wait_ge(vs.sem, vs.v)
                nc.sync.wait_ge(s_gsc.sem, s_gsc.v)
                nc.sync.dma_start(dbg["d_mask"][:, :], mask_fm[:, :]).then_inc(qd.sem, 16)
                nc.sync.dma_start(dbg["d_pm"][:, :], pm_tm[:, :]).then_inc(qd.sem, 16)
                nc.sync.dma_start(dbg["d_dest"][:, :], dest_i[:, :]).then_inc(qd.sem, 16)
                nc.sync.dma_start(dbg["d_g"][:, :], g_d[:, :]).then_inc(qd.sem, 16)
                qd.inc(64)
            nc.gpsimd.wait_ge(s_gsc.sem, s_gsc.v)
            nc.sync.wait_ge(s_lgl.sem, s_lgl.v)
            nc.all_engine_barrier()

        with ExitStack() as pd_:
            psb = lambda n, sh, d: pd_.enter_context(nc.sbuf_tensor(n, sh, d))
            pps = lambda n, sh, d: pd_.enter_context(nc.psum_tensor(n, sh, d))
            g_sb = psb("g_sb", [128, 64], I32)
            w2buf = psb("w2buf", [128, 32 * 1024], BF16)
            xg_tm = psb("xg_tm", [128, 8 * 1024], BF16)
            xg_fm = psb("xg_fm", [128, 8 * 1024], BF16)
            mh = psb("mh", [128, 32 * 512], BF16)
            y_sb = [psb(f"y_sb{i}", [128, 1024], BF16) for i in range(2)]
            yev_done = [0, 0]
            y_guard = [0, 0]
            b2r = psb("b2r", [1, 2048], F32)
            psT3 = [pps(f"psT3_{i}", [128, 1024], BF16) for i in range(2)]
            psH = [pps(f"psH{i}", [128, 512], F32) for i in range(2)]
            psY = [pps(f"psY{i}", [128, 1024], F32) for i in range(2)]

            nc.sync.dma_start(g_sb[:, :],
                              g_d[:, :].rearrange("(p c) o -> p (c o)", p=128)).then_inc(s_gld.sem, 16)
            GSB = s_gld.inc(16)
            nc.gpsimd.wait_ge(s_gld.sem, GSB)
            nc.tensor.wait_ge(s_gld.sem, GSB)
            nc.tensor.wait_ge(qd.sem, CONSTS_ALL)
            nc.scalar.wait_ge(qd.sem, CONSTS_ALL)
            nc.vector.memset(xg_tm[:, :], 0.0).then_inc(vs.sem, 1)
            XGZ = vs.inc()
            nc.gpsimd.wait_ge(vs.sem, XGZ)
            bc_d = pd_.enter_context(nc.gpsimd.register("bc_d"))
            nc.gpsimd.reg_mov(bc_d, TC - 1)

            ysc_guard = 0
            b2_guard = [0, 0]
            w1_ready = [0] * (E + 1)
            w2_ready = [0] * (E + 1)
            xg_ready = [0] * (E + 1)
            b2_ready = [0] * (E + 1)

            def load_w1(e):
                nc.gpsimd.dma_start(
                    w1buf[:, :].rearrange("p (kc f) -> p kc f", kc=8),
                    w1[e].rearrange("(kc p) f -> kc p f", p=128).transpose([1, 0, 2]),
                ).then_inc(s_w1.sem, 16)
                s_w1.inc(16)
                w1_ready[e] = s_w1.v

            def load_w2(e):
                for g4 in range(4):
                    nc.gpsimd.dma_start(
                        w2buf[:, g4 * 8192:(g4 + 1) * 8192].rearrange("p (fj f) -> p fj f", fj=8),
                        w2[e, g4 * 1024:(g4 + 1) * 1024, :].rearrange("(fj p) f -> fj p f", p=128).transpose([1, 0, 2]),
                    ).then_inc(s_w2.sem, 16)
                    s_w2.inc(16)
                w2_ready[e] = s_w2.v

            def load_xg(e):
                for rb in range(8):
                    nc.gpsimd.indirect_dma_start(
                        xg_tm[:, rb * 1024:(rb + 1) * 1024], None,
                        xn_tm_d[:, :], IOA(g_sb[:, e * 8 + rb: e * 8 + rb + 1], 0),
                        bounds_check=bc_d, oob_is_err=False,
                    ).then_inc(s_xg.sem, 16)
                    s_xg.inc(16)
                xg_ready[e] = s_xg.v

            def load_b2(e):
                sl = e % 2
                if b2_guard[sl]:
                    nc.sync.wait_ge(ts.sem, b2_guard[sl])
                nc.sync.dma_start(b2r[:, sl * 1024:(sl + 1) * 1024],
                                  b2_rows[e:e + 1, :]).then_inc(s_b2[sl].sem, 16)
                b2_ready[e] = s_b2[sl].inc(16)

            w1_ready[0] = W1_0
            load_b2(0)
            load_xg(0)
            load_w2(0)
            for e in range(E):
                sl = e % 2
                nc.tensor.wait_ge(s_xg.sem, xg_ready[e])
                t3_done = [0, 0]
                for rb in range(8):
                    ri = rb % 2
                    if t3_done[ri]:
                        nc.tensor.wait_ge(vs.sem, t3_done[ri])
                    mm = None
                    for ch in range(8):
                        mm = nc.tensor.transpose(psT3[ri][:, ch * 128:(ch + 1) * 128],
                                                 xg_tm[:, rb * 1024 + ch * 128: rb * 1024 + (ch + 1) * 128],
                                                 ident_sb[:, :])
                    mm.then_inc(ts.sem, 1)
                    XT = ts.inc()
                    nc.vector.wait_ge(ts.sem, XT)
                    out_ap = xg_fm[:, :].rearrange("p (c t) -> p c t", c=8)[:, :, rb * 128:(rb + 1) * 128]
                    in_ap = psT3[ri][:, :].rearrange("p (c t) -> p c t", c=8)
                    nc.vector.tensor_copy(out_ap, in_ap).then_inc(vs.sem, 1)
                    t3_done[ri] = vs.inc()
                XGF = vs.v
                TRD = ts.v
                if e + 1 < E:
                    nc.gpsimd.wait_ge(ts.sem, TRD)
                    load_xg(e + 1)
                    load_b2(e + 1)
                nc.tensor.wait_ge(vs.sem, XGF)
                nc.tensor.wait_ge(s_w1.sem, w1_ready[e])
                gelu_done = [0, 0]
                for csub in range(2):
                    for fj in range(32):
                        pi = fj % 2
                        if gelu_done[pi]:
                            nc.tensor.wait_ge(ss.sem, gelu_done[pi])
                        mm = None
                        for kc in range(8):
                            mm = nc.tensor.matmul(
                                psH[pi][:, :],
                                w1buf[:, kc * 4096 + fj * 128: kc * 4096 + (fj + 1) * 128],
                                xg_fm[:, kc * 1024 + csub * 512: kc * 1024 + (csub + 1) * 512],
                                start=(kc == 0), stop=(kc == 7))
                        mm.then_inc(ts.sem, 1)
                        HMM = ts.inc()
                        nc.scalar.wait_ge(ts.sem, HMM)
                        nc.scalar.activation(mh[:, fj * 512:(fj + 1) * 512], psH[pi][:, :],
                                             AF.Gelu_apprx_tanh,
                                             bias=b1c_sb[:, e * 32 + fj: e * 32 + fj + 1]).then_inc(ss.sem, 1)
                        gelu_done[pi] = ss.inc()
                    if csub == 1 and e + 1 < E:
                        nc.gpsimd.wait_ge(ts.sem, ts.v)
                        load_w1(e + 1)
                    MHD = ss.v
                    nc.tensor.wait_ge(ss.sem, MHD)
                    if csub == 0:
                        nc.tensor.wait_ge(s_w2.sem, w2_ready[e])
                        nc.tensor.wait_ge(s_b2[sl].sem, b2_ready[e])
                    for sb4 in range(4):
                        yi = sb4 % 2
                        if yev_done[yi]:
                            nc.tensor.wait_ge(ss.sem, yev_done[yi])
                        for fj in range(32):
                            nc.tensor.matmul(psY[yi][:, 0:512],
                                             mh[:, fj * 512 + sb4 * 128: fj * 512 + (sb4 + 1) * 128],
                                             w2buf[:, fj * 1024: fj * 1024 + 512],
                                             start=(fj == 0), stop=False)
                            nc.tensor.matmul(psY[yi][:, 512:1024],
                                             mh[:, fj * 512 + sb4 * 128: fj * 512 + (sb4 + 1) * 128],
                                             w2buf[:, fj * 1024 + 512: fj * 1024 + 1024],
                                             start=(fj == 0), stop=False)
                        nc.tensor.matmul(psY[yi][:, 0:512], onescol_sb[:, :], b2r[:, sl * 1024: sl * 1024 + 512],
                                         start=False, stop=False)
                        mm = nc.tensor.matmul(psY[yi][:, 512:1024], onescol_sb[:, :],
                                              b2r[:, sl * 1024 + 512: sl * 1024 + 1024],
                                              start=False, stop=True)
                        mm.then_inc(ts.sem, 1)
                        YMM = ts.inc()
                        nc.scalar.wait_ge(ts.sem, YMM)
                        if y_guard[yi]:
                            nc.scalar.wait_ge(s_sc[yi].sem, y_guard[yi])
                        YEV = sflush(nc.scalar.activation(y_sb[yi][:, :], psY[yi][:, :], AF.Identity))
                        yev_done[yi] = YEV
                        nc.gpsimd.wait_ge(ss.sem, YEV)
                        nc.gpsimd.indirect_dma_start(
                            yffn_d[:, :],
                            IOA(g_sb[:, e * 8 + csub * 4 + sb4: e * 8 + csub * 4 + sb4 + 1], 0),
                            y_sb[yi][:, :], None,
                            bounds_check=bc_d, oob_is_err=False,
                        ).then_inc(s_sc[yi].sem, 16)
                        y_guard[yi] = s_sc[yi].inc(16)
                b2_guard[sl] = ts.v
                if e + 1 < E:
                    nc.gpsimd.wait_ge(ts.sem, ts.v)
                    load_w2(e + 1)
            for c_ in [s_w1, s_w2, s_xg, s_sc[0], s_sc[1]]:
                nc.gpsimd.wait_ge(c_.sem, c_.v)
            for c_ in [s_b2[0], s_b2[1], s_gld]:
                nc.sync.wait_ge(c_.sem, c_.v)
            nc.all_engine_barrier()

        w1buf_ctx.close()
        if debug:
            nc.sync.dma_start(dbg["d_partial"][:, :], partial_d[:, :]).then_inc(qd.sem, 16)
            nc.sync.dma_start(dbg["d_yffn"][:, :], yffn_d[:, :]).then_inc(qd.sem, 16)
            qd.inc(32)
            nc.sync.wait_ge(qd.sem, qd.v)
        with ExitStack() as pe:
            psb = lambda n, sh, d: pe.enter_context(nc.sbuf_tensor(n, sh, d))
            NB_E = 4
            part_b = [psb(f"part_b{i}", [128, 1024], BF16) for i in range(NB_E)]
            yf_b = [psb(f"yf_b{i}", [128, 1024], BF16) for i in range(NB_E)]
            t1 = [psb(f"t1_{i}", [128, 1024], F32) for i in range(2)]
            o_sb = [psb(f"o_sb{i}", [128, 1024], F32) for i in range(NB_E)]
            in_done = [0] * NBLK
            cons_done = [0] * NBLK
            out_done = [0] * NBLK

            def issue_load(b):
                bi = b % NB_E
                if b >= NB_E and cons_done[b - NB_E]:
                    nc.sync.wait_ge(vs.sem, cons_done[b - NB_E])
                nc.sync.dma_start(part_b[bi][:, :], partial_d[b * 128:(b + 1) * 128, :]).then_inc(s_e[bi].sem, 16)
                nc.sync.dma_start(yf_b[bi][:, :], yffn_d[b * 128:(b + 1) * 128, :]).then_inc(s_e[bi].sem, 16)
                in_done[b] = s_e[bi].inc(32)

            for b in range(min(NB_E, NBLK)):
                issue_load(b)
            for b in range(NBLK):
                bi = b % NB_E
                nc.vector.wait_ge(s_e[bi].sem, in_done[b])
                if b >= NB_E:
                    nc.vector.wait_ge(s_eo[bi].sem, out_done[b - NB_E])
                vedge(nc.vector.tensor_scalar_mul(t1[b % 2][:, :], yf_b[bi][:, :], pm_tm[:, b:b + 1]))
                cons_done[b] = vflush(nc.vector.tensor_tensor(o_sb[bi][:, :], t1[b % 2][:, :],
                                                              part_b[bi][:, :], ALU.add))
                nc.sync.wait_ge(vs.sem, cons_done[b])
                nc.sync.dma_start(out_d[b * 128:(b + 1) * 128, :], o_sb[bi][:, :]).then_inc(s_eo[bi].sem, 16)
                out_done[b] = s_eo[bi].inc(16)
                if b + NB_E < NBLK:
                    issue_load(b + NB_E)
            for c_ in s_eo + s_e:
                nc.sync.wait_ge(c_.sem, c_.v)
            nc.all_engine_barrier()

    return nc


def _host_consts():
    ident = np.eye(128, dtype=bfdt)
    i8 = np.eye(8, dtype=np.float32)
    ones8 = np.ones((8, 1), dtype=np.float32)
    onescol = np.ones((1, 128), dtype=np.float32)
    iota_tm = (np.arange(128)[:, None] + 128 * np.arange(32)[None, :]).astype(np.int32)
    idxM = np.zeros((128, 128), dtype=np.int32)
    for h in range(16):
        for di, d in enumerate(DSET):
            idxM[:, h * 8 + di] = h * WLEN + 127 + 128 * d - np.arange(128)
    e8 = (np.arange(8)[:, None] * 8).astype(np.float32)
    return ident, i8, ones8, onescol, iota_tm, idxM, e8


_prog_cache = {}


def kernel(L0_attn_x, L0_attn_Wqkv, L0_attn_bqkv, L0_attn_Wout, L0_attn_bout,
           L0_attn_Wgate, L0_attn_bgate, L0_attn_pos_bias,
           L0_ffn_switch_w, L0_ffn_switch_b, L0_ffn_w1, L0_ffn_b1,
           L0_ffn_w2, L0_ffn_b2):
    x = np.ascontiguousarray(np.asarray(L0_attn_x, dtype=np.float32))
    Wqkv = np.ascontiguousarray(np.asarray(L0_attn_Wqkv, dtype=np.float32))
    bqkv = np.asarray(L0_attn_bqkv, dtype=np.float32)
    Wout = np.ascontiguousarray(np.asarray(L0_attn_Wout, dtype=np.float32))
    bout = np.asarray(L0_attn_bout, dtype=np.float32)
    Wgate = np.ascontiguousarray(np.asarray(L0_attn_Wgate, dtype=np.float32))
    bgate = np.asarray(L0_attn_bgate, dtype=np.float32)
    pb = np.asarray(L0_attn_pos_bias, dtype=np.float32)
    sww = np.ascontiguousarray(np.asarray(L0_ffn_switch_w, dtype=np.float32))
    swb = np.asarray(L0_ffn_switch_b, dtype=np.float32)
    w1 = np.ascontiguousarray(np.asarray(L0_ffn_w1, dtype=np.float32))
    b1 = np.asarray(L0_ffn_b1, dtype=np.float32)
    w2 = np.ascontiguousarray(np.asarray(L0_ffn_w2, dtype=np.float32))
    b2 = np.ascontiguousarray(np.asarray(L0_ffn_b2, dtype=np.float32))

    bqkv_cols = np.ascontiguousarray(bqkv[:2048].reshape(16, 128).T)
    bv_row = np.ascontiguousarray(bqkv[2048:3072].reshape(1, D))
    bg_row = np.ascontiguousarray(bgate.reshape(1, D))
    bo_row = np.ascontiguousarray(bout.reshape(1, D))
    swb_col = np.ascontiguousarray(swb.reshape(8, 1))
    b1_cols = np.ascontiguousarray(b1.reshape(E * 32, 128).T)
    pbT = np.ascontiguousarray(pb.T)

    ident, i8, ones8, onescol, iota_tm, idxM, e8 = _host_consts()

    if "nc" not in _prog_cache:
        _prog_cache["nc"] = build_program()
    nc = _prog_cache["nc"]

    shared = dict(Wqkv=Wqkv, Wgate=Wgate, Wout=Wout, sww=sww, w1=w1, w2=w2,
                  bqkv_cols=bqkv_cols, bv_row=bv_row, bg_row=bg_row, bo_row=bo_row,
                  swb_col=swb_col, b1_cols=b1_cols, b2_rows=b2, pbT=pbT,
                  ident_bf=ident, i8c=i8, ones8=ones8, onescol=onescol,
                  iota_tm_c=iota_tm, idxM=idxM, e8col=e8)
    in_maps = []
    for core in range(NCORES):
        shard = x[core * SEQ_PER_CORE:(core + 1) * SEQ_PER_CORE].reshape(TC, D)
        m = dict(shared)
        m["x"] = np.ascontiguousarray(shard)
        in_maps.append(m)

    res = run_bass_kernel_spmd(nc, in_maps, core_ids=list(range(NCORES)))
    outs = [np.asarray(r["out"], dtype=np.float32).reshape(SEQ_PER_CORE, N, D)
            for r in res.results]
    return np.concatenate(outs, axis=0)


# version 19
# speedup vs baseline: 1.0853x; 1.0853x over previous
import numpy as np
import ml_dtypes
from contextlib import ExitStack

import concourse.bass as bass
import concourse.mybir as mybir
from concourse.bass_utils import run_bass_kernel_spmd

F32 = mybir.dt.float32
BF16 = mybir.dt.bfloat16
I32 = mybir.dt.int32
AX = mybir.AxisListType
ALU = mybir.AluOpType
AF = mybir.ActivationFunctionType
IOA = bass.IndirectOffsetOnAxis
bfdt = ml_dtypes.bfloat16

NCORES = 8
B, N, D, H, HD = 16, 2048, 1024, 16, 64
E, F = 8, 4096
SEQ_PER_CORE = B // NCORES
TC = SEQ_PER_CORE * N
C = 896
RB = C // 128
TOTC = E * C // 128
CSP = [(0, 512), (512, C - 512)]
NBLK = TC // 128
OFFSETS = sorted(set(range(0, 33)) | {48, 64, 96, 128, 192, 256, 384, 512, 768, 1024, 1536})
DSET = [0, 1, 2, 3, 4, 6, 8, 12]
WLEN = 1792
SCALE = HD ** -0.5
EXP_SHIFT = -3.0


class Ctr:
    def __init__(self, sem):
        self.sem = sem
        self.v = 0

    def inc(self, n=1):
        self.v += n
        return self.v


def build_program(debug=False):
    nc = bass.Bass("TRN2", target_bir_lowering=False)

    x_in = nc.dram_tensor("x", [TC, D], F32, kind="ExternalInput")
    Wqkv = nc.dram_tensor("Wqkv", [D, 3 * D], F32, kind="ExternalInput")
    Wgate = nc.dram_tensor("Wgate", [D, D], F32, kind="ExternalInput")
    Wout = nc.dram_tensor("Wout", [D, D], F32, kind="ExternalInput")
    sww = nc.dram_tensor("sww", [D, E], F32, kind="ExternalInput")
    w1 = nc.dram_tensor("w1", [E, D, F], F32, kind="ExternalInput")
    w2 = nc.dram_tensor("w2", [E, F, D], F32, kind="ExternalInput")
    bqkv_cols = nc.dram_tensor("bqkv_cols", [128, 16], F32, kind="ExternalInput")
    bv_row = nc.dram_tensor("bv_row", [1, D], F32, kind="ExternalInput")
    bg_row = nc.dram_tensor("bg_row", [1, D], F32, kind="ExternalInput")
    bo_row = nc.dram_tensor("bo_row", [1, D], F32, kind="ExternalInput")
    swb_col = nc.dram_tensor("swb_col", [8, 1], F32, kind="ExternalInput")
    b1_cols = nc.dram_tensor("b1_cols", [128, 256], F32, kind="ExternalInput")
    b2_rows = nc.dram_tensor("b2_rows", [E, D], F32, kind="ExternalInput")
    pbT = nc.dram_tensor("pbT", [16, 44], F32, kind="ExternalInput")
    ident_bf = nc.dram_tensor("ident_bf", [128, 128], BF16, kind="ExternalInput")
    i8c = nc.dram_tensor("i8c", [8, 8], F32, kind="ExternalInput")
    ones8 = nc.dram_tensor("ones8", [8, 1], F32, kind="ExternalInput")
    onescol = nc.dram_tensor("onescol", [1, 128], F32, kind="ExternalInput")
    iota_tm_c = nc.dram_tensor("iota_tm_c", [128, 32], I32, kind="ExternalInput")
    idxM = nc.dram_tensor("idxM", [128, 128], I32, kind="ExternalInput")
    e8col = nc.dram_tensor("e8col", [8, 1], F32, kind="ExternalInput")

    out_d = nc.dram_tensor("out", [TC, D], F32, kind="ExternalOutput")
    if debug:
        dbg = {}
        for nm, shp, dt in [("d_xn", [TC, D], BF16), ("d_k", [128, 4096], BF16),
                            ("d_v65", [128, 2080], BF16), ("d_q", [4, 8, 128, 128], BF16),
                            ("d_gate", [TC, D], BF16), ("d_logits", [8, TC], F32),
                            ("d_P1", [128, 2048], BF16), ("d_P2", [128, 2048], BF16),
                            ("d_flat", [128, 1024], BF16), ("d_M", [128, 2048], BF16),
                            ("d_mask", [8, TC], F32), ("d_pm", [128, 32], F32),
                            ("d_dest", [128, 32], I32), ("d_g", [E * C, 1], I32),
                            ("d_yffn", [TC, D], BF16), ("d_partial", [TC, D], BF16),
                            ("d_xg", [128, RB * 1024], BF16), ("d_stat", [128, 10], F32)]:
            dbg[nm] = nc.dram_tensor(nm, shp, dt, kind="ExternalOutput")

    xn_tm_d = nc.dram_tensor("xn_tm_d", [TC, D], BF16, kind="Internal")
    q_d = nc.dram_tensor("q_d", [SEQ_PER_CORE, 16, 8, 128, 128], BF16, kind="Internal")
    gate_d = nc.dram_tensor("gate_d", [TC, D], BF16, kind="Internal")
    partial_d = nc.dram_tensor("partial_d", [TC, D], BF16, kind="Internal")
    yffn_d = nc.dram_tensor("yffn_d", [TC, D], BF16, kind="Internal")
    w_toe_d = nc.dram_tensor("w_toe_d", [16, WLEN], BF16, kind="Internal")
    g_d = nc.dram_tensor("g_d", [E * C, 1], I32, kind="Internal")
    logits_d = nc.dram_tensor("logits_d", [8, TC], F32, kind="Internal")

    with ExitStack() as top:
        mksem = lambda n: Ctr(top.enter_context(nc.semaphore(n)))
        qd = mksem("qd")
        gd = mksem("gd")
        ts = mksem("ts")
        ss = mksem("ss")
        vs = mksem("vs")
        s_x = mksem("s_x")
        s_xnst = [mksem(f"s_xnst{i}") for i in range(2)]
        s_qst = [mksem(f"s_qst{i}") for i in range(2)]
        s_gst = mksem("s_gst")
        s_lgst = [mksem(f"s_lgst{i}") for i in range(2)]
        s_wt = mksem("s_wt")
        s_z = mksem("s_z")
        s_bld = [mksem(f"s_bld{i}") for i in range(2)]
        s_gx = mksem("s_gx")
        s_pst = mksem("s_pst")
        s_lgl = mksem("s_lgl")
        s_gsc = mksem("s_gsc")
        s_gld = mksem("s_gld")
        s_b2 = [mksem(f"s_b2_{i}") for i in range(2)]
        s_wq = mksem("s_wq")
        s_wv = mksem("s_wv")
        s_m = mksem("s_m")
        s_wg = mksem("s_wg")
        s_w1 = mksem("s_w1")
        s_w2 = mksem("s_w2")
        s_xg = mksem("s_xg")
        s_sc = [mksem(f"s_sc{i}") for i in range(2)]
        s_e = [mksem(f"s_e{i}") for i in range(4)]
        s_eo = [mksem(f"s_eo{i}") for i in range(4)]

        def vedge(inst):
            inst.then_inc(vs.sem, 1)
            nc.vector.wait_ge(vs.sem, vs.inc())

        def vflush(inst):
            vedge(inst)
            nc.vector.nop().then_inc(vs.sem, 1)
            return vs.inc()

        def sflush(inst):
            inst.then_inc(ss.sem, 1)
            nc.scalar.wait_ge(ss.sem, ss.inc())
            nc.scalar.nop().then_inc(ss.sem, 1)
            return ss.inc()

        sb = lambda n, s, d: top.enter_context(nc.sbuf_tensor(n, s, d))
        ident_sb = sb("ident_sb", [128, 128], BF16)
        i8_sb = sb("i8_sb", [8, 8], F32)
        ones8_sb = sb("ones8_sb", [8, 1], F32)
        onescol_sb = sb("onescol_sb", [1, 128], F32)
        iota_sb = sb("iota_sb", [128, 32], I32)
        idxM_sb = sb("idxM_sb", [128, 128], I32)
        e8_sb = sb("e8_sb", [8, 1], F32)
        bqkvc_sb = sb("bqkvc_sb", [128, 16], F32)
        swb_sb = sb("swb_sb", [8, 1], F32)
        b1c_sb = sb("b1c_sb", [128, 256], F32)
        pbT_sb = sb("pbT_sb", [16, 44], F32)
        epb_sb = sb("epb_sb", [16, 44], F32)
        pm_tm = sb("pm_tm", [128, 32], F32)

        for dst, src in [(ident_sb, ident_bf), (i8_sb, i8c), (ones8_sb, ones8),
                         (onescol_sb, onescol), (iota_sb, iota_tm_c), (idxM_sb, idxM),
                         (e8_sb, e8col), (bqkvc_sb, bqkv_cols), (swb_sb, swb_col),
                         (b1c_sb, b1_cols), (pbT_sb, pbT)]:
            nc.sync.dma_start(dst[:, :], src[:, :]).then_inc(qd.sem, 16)
            qd.inc(16)
        CONSTS = qd.v

        with ExitStack() as attn:
            asb = lambda n, s, d: attn.enter_context(nc.sbuf_tensor(n, s, d))
            Wqkv_bf = asb("Wqkv_bf", [128, 8 * 3072], BF16)
            WgWo_bf = asb("WgWo_bf", [128, 8 * 1024], BF16)
            sw_bf = asb("sw_bf", [128, 8 * 8], BF16)
            M_all = asb("M_all", [128, 16 * 8 * 128], BF16)
            k_fm = asb("k_fm", [128, 8 * 2048], BF16)
            v65 = asb("v65", [128, 16 * 1040], BF16)
            bv_sb = asb("bv_sb", [1, D], F32)
            bg_sb = asb("bg_sb", [1, D], F32)
            bo_sb = asb("bo_sb", [1, D], F32)
            for dst, src in [(bv_sb, bv_row), (bg_sb, bg_row), (bo_sb, bo_row)]:
                nc.sync.dma_start(dst[:, :], src[:, :]).then_inc(qd.sem, 16)
                qd.inc(16)
            CONSTS_ALL = qd.v

            for kc in range(8):
                nc.gpsimd.dma_start(WgWo_bf[:, kc * 1024:(kc + 1) * 1024],
                                    Wgate[kc * 128:(kc + 1) * 128, :]).then_inc(s_wg.sem, 16)
                s_wg.inc(16)
            WG_L0 = s_wg.v
            for kc in range(8):
                nc.gpsimd.dma_start(sw_bf[:, kc * 8:(kc + 1) * 8],
                                    sww[kc * 128:(kc + 1) * 128, :]).then_inc(s_wv.sem, 16)
                s_wv.inc(16)
            for kc in range(8):
                nc.gpsimd.dma_start(Wqkv_bf[:, kc * 3072 + 2048:(kc + 1) * 3072],
                                    Wqkv[kc * 128:(kc + 1) * 128, 2048:3072]).then_inc(s_wv.sem, 16)
                s_wv.inc(16)
            WV_L = s_wv.v
            for kc in range(8):
                nc.gpsimd.dma_start(Wqkv_bf[:, kc * 3072: kc * 3072 + 2048],
                                    Wqkv[kc * 128:(kc + 1) * 128, 0:2048]).then_inc(s_wq.sem, 16)
                s_wq.inc(16)
            WQKV_L = s_wq.v

            wtoe_ctx = ExitStack()
            w_toe = wtoe_ctx.enter_context(nc.sbuf_tensor("w_toe", [16, WLEN], BF16))
            nc.scalar.wait_ge(qd.sem, CONSTS)
            nc.scalar.activation(epb_sb[:, :], pbT_sb[:, :], AF.Exp).then_inc(ss.sem, 1)
            EPB = ss.inc()
            vedge(nc.vector.memset(w_toe[:, :], 0.0))
            nc.vector.wait_ge(ss.sem, EPB)
            last = None
            for i, dlt in enumerate(OFFSETS):
                last = nc.vector.tensor_copy(w_toe[:, 127 + dlt:128 + dlt], epb_sb[:, i:i + 1])
            WTOE = vflush(last)
            nc.sync.wait_ge(vs.sem, WTOE)
            nc.sync.dma_start(w_toe_d[:, :], w_toe[:, :]).then_inc(s_wt.sem, 16)
            WTOE_ST = s_wt.inc(16)
            with nc.sbuf_tensor("zb", [128, 1024], BF16) as zb, \
                 nc.sbuf_tensor("pois", [128, TOTC], I32) as pois:
                nc.vector.memset(zb[:, :], 0.0)
                ZME = vflush(nc.vector.memset(pois[:, :], 1 << 20))
                nc.sync.wait_ge(vs.sem, ZME)
                for b in range(NBLK):
                    nc.sync.dma_start(yffn_d[b * 128:(b + 1) * 128, :], zb[:, :]).then_inc(s_z.sem, 16)
                    s_z.inc(16)
                nc.sync.dma_start(g_d[:, :].rearrange("(p c) o -> p (c o)", p=128),
                                  pois[:, :]).then_inc(s_z.sem, 16)
                ZD = s_z.inc(16)
                nc.sync.wait_ge(s_z.sem, ZD)
                nc.all_engine_barrier()
            wtoe_ctx.close()
            nc.gpsimd.wait_ge(s_wt.sem, WTOE_ST)
            for h in range(16):
                for di in range(8):
                    col = h * 8 + di
                    nc.gpsimd.indirect_dma_start(
                        M_all[:, col * 128:(col + 1) * 128], None,
                        w_toe_d[:, :], IOA(idxM_sb[:, col:col + 1], 1),
                    ).then_inc(s_m.sem, 16)
                    s_m.inc(16)
            M_L = s_m.v

            for s in range(SEQ_PER_CORE):
                ts_base = s * N
                with ExitStack() as pa:
                    psb = lambda n, sh, d: pa.enter_context(nc.sbuf_tensor(f"{n}_s{s}", sh, d))
                    pps = lambda n, sh, d: pa.enter_context(nc.psum_tensor(f"{n}_s{s}", sh, d))
                    xn_half = psb("xn_half", [128, 8 * 1024], BF16)
                    x_sb = [psb(f"x_sb{i}", [128, 1024], F32) for i in range(2)]
                    xn_tm = [psb(f"xn_tm{i}", [128, 1024], BF16) for i in range(2)]
                    sq_scr = psb("sq_scr", [128, 256], F32)
                    stat = psb("stat", [128, 10], F32)
                    lg_ev = [psb(f"lg_ev{i}", [8, 128], F32) for i in range(2)]
                    qk_ev = [psb(f"qk_ev{i}", [128, 512], BF16) for i in range(2)]
                    psT = pps("psT", [128, 1024], BF16)
                    psQK = [pps(f"psQK{i}", [128, 512], F32) for i in range(2)]
                    psV = pps("psV", [128, 1024], F32)
                    psG = pps("psG", [128, 1024], F32)
                    psL = pps("psL", [8, 128], F32)

                    if s == 0:
                        WG_L = WG_L0
                    else:
                        nc.gpsimd.wait_ge(ts.sem, ts.v)
                        for kc in range(8):
                            nc.gpsimd.dma_start(WgWo_bf[:, kc * 1024:(kc + 1) * 1024],
                                                Wgate[kc * 128:(kc + 1) * 128, :]).then_inc(s_wg.sem, 16)
                            s_wg.inc(16)
                        WG_L = s_wg.v

                    v65v = v65[:, :].rearrange("p (b h r) -> p (b h) r", h=16, r=65)
                    nc.vector.wait_ge(ts.sem, ts.v)
                    nc.vector.memset(v65v[:, :, 64:65], 1.0).then_inc(vs.sem, 1)
                    vs.inc()

                    xg = [[], []]
                    xng = [[], []]
                    last_evac = None
                    last_XN = None
                    if s == 0:
                        gt_guard = 0
                        lg_guard = [0, 0]
                        vg_state = {"vs": 0, "ss": 0}
                    nc.tensor.wait_ge(qd.sem, CONSTS_ALL)
                    nc.tensor.wait_ge(s_wv.sem, WV_L)
                    nc.tensor.wait_ge(s_wg.sem, WG_L)
                    ev_of = {}
                    xn_extra = [[], []]
                    xnst_val = [0, 0]

                    def emit_vg(half_, blk_, EV_):
                        nonlocal gt_guard
                        vb = half_ * 8 + blk_
                        g0v = ts_base + half_ * 1024 + blk_ * 128
                        nc.tensor.wait_ge(vs.sem, max(EV_, vg_state["vs"]))
                        if vg_state["ss"]:
                            nc.tensor.wait_ge(ss.sem, vg_state["ss"])
                        for kc in range(8):
                            lhs = xn_half[:, kc * 1024 + blk_ * 128: kc * 1024 + (blk_ + 1) * 128]
                            st = (kc == 0)
                            nc.tensor.matmul(psV[:, 0:512], lhs,
                                             Wqkv_bf[:, kc * 3072 + 2048: kc * 3072 + 2560],
                                             start=st, stop=False)
                            nc.tensor.matmul(psV[:, 512:1024], lhs,
                                             Wqkv_bf[:, kc * 3072 + 2560: kc * 3072 + 3072],
                                             start=st, stop=False)
                            nc.tensor.matmul(psG[:, 0:512], lhs,
                                             WgWo_bf[:, kc * 1024: kc * 1024 + 512],
                                             start=st, stop=False)
                            nc.tensor.matmul(psG[:, 512:1024], lhs,
                                             WgWo_bf[:, kc * 1024 + 512: kc * 1024 + 1024],
                                             start=st, stop=False)
                            nc.tensor.matmul(psL[:, :], sw_bf[:, kc * 8:(kc + 1) * 8], lhs,
                                             start=st, stop=(kc == 7))
                        nc.tensor.matmul(psV[:, 0:512], onescol_sb[:, :], bv_sb[:, 0:512],
                                         start=False, stop=False)
                        nc.tensor.matmul(psV[:, 512:1024], onescol_sb[:, :], bv_sb[:, 512:1024],
                                         start=False, stop=True)
                        nc.tensor.matmul(psG[:, 0:512], onescol_sb[:, :], bg_sb[:, 0:512],
                                         start=False, stop=False)
                        mm_ = nc.tensor.matmul(psG[:, 512:1024], onescol_sb[:, :], bg_sb[:, 512:1024],
                                               start=False, stop=True)
                        mm_.then_inc(ts.sem, 1)
                        VG = ts.inc()
                        nc.vector.wait_ge(ts.sem, VG)
                        vo = v65[:, vb * 1040:(vb + 1) * 1040].rearrange("p (h r) -> p h r", r=65)[:, :, 0:64]
                        vi = psV[:, :].rearrange("p (h r) -> p h r", r=64)
                        nc.vector.tensor_copy(vo, vi).then_inc(vs.sem, 1)
                        vg_state["vs"] = vs.inc()
                        nc.scalar.wait_ge(ts.sem, VG)
                        gt_ev = xn_tm[blk_ % 2]
                        if gt_guard:
                            nc.scalar.wait_ge(s_gst.sem, gt_guard)
                        nc.scalar.wait_ge(s_xnst[blk_ % 2].sem, xnst_val[blk_ % 2])
                        GT = sflush(nc.scalar.activation(gt_ev[:, :], psG[:, :], AF.Sigmoid))
                        lbi = blk_ % 2
                        if lg_guard[lbi]:
                            nc.scalar.wait_ge(s_lgst[lbi].sem, lg_guard[lbi])
                        LGE = sflush(nc.scalar.activation(lg_ev[lbi][:, :], psL[:, :], AF.Identity,
                                                          bias=swb_sb[:, 0:1]))
                        vg_state["ss"] = ss.v
                        nc.sync.wait_ge(ss.sem, GT)
                        nc.sync.dma_start(gate_d[g0v:g0v + 128, :], gt_ev[:, :]).then_inc(s_gst.sem, 16)
                        gt_guard = s_gst.inc(16)
                        xn_extra[blk_ % 2] = [(s_gst.sem, gt_guard), (ss.sem, GT)]
                        nc.sync.wait_ge(ss.sem, LGE)
                        nc.sync.dma_start(logits_d[:, g0v:g0v + 128], lg_ev[lbi][:, :]).then_inc(s_lgst[lbi].sem, 16)
                        lg_guard[lbi] = s_lgst[lbi].inc(16)
                    for half in range(2):
                        h0 = ts_base + half * 1024
                        for blk in range(8):
                            g0 = h0 + blk * 128
                            bi = blk % 2
                            for gsem, gval in xg[bi]:
                                nc.sync.wait_ge(gsem, gval)
                            nc.sync.dma_start(x_sb[bi][:, :], x_in[g0:g0 + 128, :]).then_inc(s_x.sem, 16)
                            XBLK = s_x.inc(16)
                            nc.vector.wait_ge(s_x.sem, XBLK)
                            nc.vector.reduce_sum(stat[:, 0:1], x_sb[bi][:, :], axis=AX.X)
                            nc.scalar.wait_ge(s_x.sem, XBLK)
                            if last_XN is not None:
                                nc.scalar.wait_ge(vs.sem, last_XN)
                            nc.scalar.activation(sq_scr[:, :], x_sb[bi][:, 0:256], AF.Square,
                                                 accum_out=stat[:, 1:2])
                            nc.scalar.activation(sq_scr[:, :], x_sb[bi][:, 256:512], AF.Square,
                                                 accum_out=stat[:, 2:3])
                            nc.scalar.activation(sq_scr[:, :], x_sb[bi][:, 512:768], AF.Square,
                                                 accum_out=stat[:, 0:1] if False else stat[:, 4:5])
                            nc.scalar.activation(sq_scr[:, :], x_sb[bi][:, 768:1024], AF.Square,
                                                 accum_out=stat[:, 5:6]).then_inc(ss.sem, 1)
                            SQ = ss.inc()
                            nc.vector.wait_ge(ss.sem, SQ)
                            nc.vector.tensor_tensor(stat[:, 6:7], stat[:, 1:2], stat[:, 2:3], ALU.add)
                            vedge(nc.vector.tensor_tensor(stat[:, 7:8], stat[:, 4:5], stat[:, 5:6], ALU.add))
                            nc.vector.tensor_scalar_mul(stat[:, 3:4], stat[:, 0:1], 1.0 / D)
                            vedge(nc.vector.tensor_tensor(stat[:, 1:2], stat[:, 6:7], stat[:, 7:8], ALU.add))
                            vedge(nc.vector.tensor_scalar_mul(stat[:, 2:3], stat[:, 1:2], 1.0 / D))
                            vedge(nc.vector.tensor_tensor(stat[:, 6:7], stat[:, 3:4], stat[:, 3:4], ALU.mult))
                            nc.vector.tensor_tensor(stat[:, 7:8], stat[:, 2:3], stat[:, 6:7],
                                                    ALU.subtract).then_inc(vs.sem, 1)
                            VAR = vs.inc()
                            nc.scalar.wait_ge(vs.sem, VAR)
                            nc.scalar.activation(stat[:, 8:9], stat[:, 7:8], AF.Sqrt,
                                                 bias=np.float32(1e-5)).then_inc(ss.sem, 1)
                            SD = ss.inc()
                            nc.vector.wait_ge(ss.sem, SD)
                            vedge(nc.vector.reciprocal(stat[:, 9:10], stat[:, 8:9]))
                            for gsem, gval in xng[bi] + xn_extra[bi]:
                                nc.vector.wait_ge(gsem, gval)
                            XN = vflush(nc.vector.tensor_scalar(xn_tm[bi][:, :], x_sb[bi][:, :],
                                                                 stat[:, 3:4], stat[:, 9:10],
                                                                 ALU.subtract, ALU.mult))
                            last_XN = XN
                            xg[bi] = [(vs.sem, XN)]
                            nc.sync.wait_ge(vs.sem, XN)
                            nc.sync.dma_start(xn_tm_d[g0:g0 + 128, :], xn_tm[bi][:, :]).then_inc(s_xnst[bi].sem, 16)
                            XNST = (s_xnst[bi].sem, s_xnst[bi].inc(16))
                            xnst_val[bi] = XNST[1]
                            nc.tensor.wait_ge(vs.sem, XN)
                            if last_evac is not None:
                                nc.tensor.wait_ge(vs.sem, last_evac)
                            for ch in range(8):
                                mm = nc.tensor.transpose(psT[:, ch * 128:(ch + 1) * 128],
                                                         xn_tm[bi][:, ch * 128:(ch + 1) * 128],
                                                         ident_sb[:, :])
                            mm.then_inc(ts.sem, 1)
                            TR = ts.inc()
                            nc.vector.wait_ge(ts.sem, TR)
                            out_ap = xn_half[:, :].rearrange("p (c t) -> p c t", c=8)[:, :, blk * 128:(blk + 1) * 128]
                            in_ap = psT[:, :].rearrange("p (c t) -> p c t", c=8)
                            nc.vector.tensor_copy(out_ap, in_ap).then_inc(vs.sem, 1)
                            EV = vs.inc()
                            last_evac = EV
                            xng[bi] = [XNST, (vs.sem, EV)]
                            if blk >= 1:
                                emit_vg(half, blk - 1, ev_of[blk - 1])
                            ev_of[blk] = EV
                        emit_vg(half, 7, ev_of[7])
                        XN_READY = vs.v

                        nc.tensor.wait_ge(s_wq.sem, WQKV_L)
                        nc.tensor.wait_ge(vs.sem, XN_READY)
                        if half == 0 and s == 0:
                            qk_guard = [0, 0]
                            qk_ev_done = [0, 0]
                        for oc in range(16):
                            for sub in range(2):
                                it = oc * 2 + sub
                                ei = it % 2
                                if qk_ev_done[ei]:
                                    nc.tensor.wait_ge(ss.sem, qk_ev_done[ei])
                                for kc in range(8):
                                    mm = nc.tensor.matmul(
                                        psQK[ei][:, :],
                                        Wqkv_bf[:, kc * 3072 + oc * 128: kc * 3072 + (oc + 1) * 128],
                                        xn_half[:, kc * 1024 + sub * 512: kc * 1024 + (sub + 1) * 512],
                                        start=(kc == 0), stop=(kc == 7))
                                mm.then_inc(ts.sem, 1)
                                MM = ts.inc()
                                nc.scalar.wait_ge(ts.sem, MM)
                                if oc < 8:
                                    if qk_guard[ei]:
                                        nc.scalar.wait_ge(s_qst[ei].sem, qk_guard[ei])
                                    QE = sflush(nc.scalar.activation(qk_ev[ei][:, :], psQK[ei][:, :], AF.Identity,
                                                                      bias=bqkvc_sb[:, oc:oc + 1]))
                                    nc.sync.wait_ge(ss.sem, QE)
                                    qb0 = half * 8 + sub * 4
                                    dst = q_d[s, qb0:qb0 + 4, oc, :, :].transpose([1, 0, 2])
                                    src = qk_ev[ei][:, :].rearrange("p (q t) -> p q t", q=4)
                                    nc.sync.dma_start(dst, src).then_inc(s_qst[ei].sem, 16)
                                    qk_guard[ei] = s_qst[ei].inc(16)
                                else:
                                    kk = oc - 8
                                    nc.scalar.activation(
                                        k_fm[:, kk * 2048 + half * 1024 + sub * 512:
                                             kk * 2048 + half * 1024 + (sub + 1) * 512],
                                        psQK[ei][:, :], AF.Identity,
                                        bias=bqkvc_sb[:, oc:oc + 1]).then_inc(ss.sem, 1)
                                    ss.inc()
                                qk_ev_done[ei] = ss.v

                    if debug and s == 0:
                        nc.sync.wait_ge(vs.sem, vs.v)
                        nc.sync.wait_ge(ss.sem, ss.v)
                        for c_ in [s_x, s_xnst[0], s_xnst[1], s_qst[0], s_qst[1],
                                   s_gst, s_lgst[0], s_lgst[1]]:
                            nc.sync.wait_ge(c_.sem, c_.v)
                        nc.sync.wait_ge(s_m.sem, s_m.v)
                        nc.sync.dma_start(dbg["d_stat"][:, :], stat[:, :]).then_inc(qd.sem, 16)
                        nc.sync.dma_start(dbg["d_k"][:, :], k_fm[:, 0:4096]).then_inc(qd.sem, 16)
                        nc.sync.dma_start(dbg["d_v65"][:, :], v65[:, 0:2080]).then_inc(qd.sem, 16)
                        nc.sync.dma_start(dbg["d_xn"][:, :], xn_tm_d[:, :]).then_inc(qd.sem, 16)
                        nc.sync.dma_start(dbg["d_q"][:, :, :, :], q_d[0, 0:4]).then_inc(qd.sem, 16)
                        nc.sync.dma_start(dbg["d_gate"][:, :], gate_d[:, :]).then_inc(qd.sem, 16)
                        nc.sync.dma_start(dbg["d_logits"][:, :], logits_d[:, :]).then_inc(qd.sem, 16)
                        nc.sync.dma_start(dbg["d_M"][:, :], M_all[:, 0:2048]).then_inc(qd.sem, 16)
                        qd.inc(8 * 16)
                        nc.sync.wait_ge(qd.sem, qd.v)
                    for c_ in [s_x, s_xnst[0], s_xnst[1], s_qst[0], s_qst[1],
                               s_gst, s_lgst[0], s_lgst[1]]:
                        nc.sync.wait_ge(c_.sem, c_.v)
                    nc.gpsimd.wait_ge(s_wg.sem, s_wg.v)
                    nc.all_engine_barrier()

                with ExitStack() as pb:
                    psb = lambda n, sh, d: pb.enter_context(nc.sbuf_tensor(f"{n}_s{s}", sh, d))
                    pps = lambda n, sh, d: pb.enter_context(nc.psum_tensor(f"{n}_s{s}", sh, d))
                    q_qb = [psb(f"q_qb{i}", [128, 1024], BF16) for i in range(2)]
                    gate_one = psb("gate_one", [128, 1024], BF16)
                    gate_qb = [gate_one, gate_one]
                    x_one = psb("x_one", [128, 1024], F32)
                    x_qb = [x_one, x_one]
                    P1 = [psb(f"P1_{i}", [128, 2048], BF16) for i in range(2)]
                    P2 = [psb(f"P2_{i}", [128, 2048], BF16) for i in range(2)]
                    flat_qb = psb("flat_qb", [128, 1024], BF16)
                    rec16 = psb("rec16", [128, 16], F32)
                    part_one = psb("part_one", [128, 1024], BF16)
                    part_ev = [part_one, part_one]
                    psSp = pps("psSp", [128, 2048], F32)
                    psS = [psSp[:, 0:1024], psSp[:, 1024:2048]]
                    psMulti = pps("psMulti", [128, 1536], F32)
                    psT2 = pps("psT2", [128, 1024], BF16)
                    pvcol = lambda u: (u // 7) * 512 + (u % 7) * 65

                    nc.gpsimd.wait_ge(ts.sem, ts.v)
                    for kc in range(8):
                        nc.gpsimd.dma_start(WgWo_bf[:, kc * 1024:(kc + 1) * 1024],
                                            Wout[kc * 128:(kc + 1) * 128, :]).then_inc(s_wg.sem, 16)
                        s_wg.inc(16)
                    WO_L = s_wg.v
                    nc.tensor.wait_ge(s_m.sem, M_L)

                    pf_guard = {}
                    gx_guard = []
                    if s == 0:
                        part_guard = [0, 0]
                    prev_PTE = 0
                    for qb in range(16):
                        bi = qb % 2
                        g0 = ts_base + qb * 128
                        for gsem, gval in pf_guard.get(bi, []):
                            nc.sync.wait_ge(gsem, gval)
                        nc.sync.dma_start(q_qb[bi][:, :],
                                          q_d[s, qb].transpose([1, 0, 2])).then_inc(s_bld[bi].sem, 16)
                        s_bld[bi].inc(16)
                        QLOAD = s_bld[bi].v
                        for gsem, gval in gx_guard:
                            nc.sync.wait_ge(gsem, gval)
                        nc.sync.dma_start(gate_qb[bi][:, :], gate_d[g0:g0 + 128, :]).then_inc(s_gx.sem, 16)
                        nc.sync.dma_start(x_qb[bi][:, :], x_in[g0:g0 + 128, :]).then_inc(s_gx.sem, 16)
                        GXLOAD = s_gx.inc(32)
                        nc.tensor.wait_ge(s_bld[bi].sem, QLOAD)
                        valid = [di for di, dd in enumerate(DSET) if qb - dd >= 0]
                        nv = len(valid)
                        exp_done = [0] * 8
                        msk_done = [0] * 8
                        pvd = [0] * 8
                        if qb == 0:
                            prev_pv_last = 0
                            if s == 0:
                                prev_msk_m1 = prev_msk_m2 = 0

                        def emit_pv(c):
                            nc.tensor.wait_ge(vs.sem, msk_done[c])
                            if c == 0:
                                nc.tensor.wait_ge(vs.sem, prev_PTE)
                            mm = None
                            for hh in range(2):
                                hg = 2 * c + hh
                                col = pvcol(hg)
                                for j, di in enumerate(valid):
                                    kb = qb - DSET[di]
                                    mm = nc.tensor.matmul(
                                        psMulti[:, col:col + 65],
                                        P2[c % 2][:, (hh * 8 + di) * 128: (hh * 8 + di + 1) * 128],
                                        v65[:, kb * 1040 + hg * 65: kb * 1040 + (hg + 1) * 65],
                                        start=(j == 0), stop=(j == len(valid) - 1))
                            mm.then_inc(ts.sem, 1)
                            return ts.inc()

                        for c in range(8):
                            if c >= 1:
                                nc.tensor.wait_ge(ss.sem, exp_done[c - 1])
                            elif qb > 0:
                                nc.tensor.wait_ge(ss.sem, ss.v)
                            mm = None
                            for di in valid:
                                kb = qb - DSET[di]
                                for hh in range(2):
                                    mm = nc.tensor.matmul(
                                        psS[hh][:, di * 128:(di + 1) * 128],
                                        k_fm[64 * hh:64 * hh + 64,
                                             c * 2048 + kb * 128: c * 2048 + (kb + 1) * 128],
                                        q_qb[bi][64 * hh:64 * hh + 64, c * 128:(c + 1) * 128],
                                        start=True, stop=True, tile_position=(64 * hh, 0))
                            mm.then_inc(ts.sem, 1)
                            SPD = ts.inc()
                            nc.scalar.wait_ge(ts.sem, SPD)
                            if c >= 2:
                                nc.scalar.wait_ge(vs.sem, msk_done[c - 2])
                            elif qb > 0:
                                nc.scalar.wait_ge(vs.sem, prev_msk_m1 if c == 1 else prev_msk_m2)
                            if nv == 8:
                                act = nc.scalar.activation(
                                    P1[c % 2][:, :], psSp[:, :],
                                    AF.Exp, bias=np.float32(EXP_SHIFT), scale=np.float32(SCALE))
                            else:
                                for hh in range(2):
                                    act = nc.scalar.activation(
                                        P1[c % 2][:, hh * 1024: hh * 1024 + nv * 128],
                                        psS[hh][:, 0: nv * 128],
                                        AF.Exp, bias=np.float32(EXP_SHIFT), scale=np.float32(SCALE))
                            act.then_inc(ss.sem, 1)
                            exp_done[c] = ss.inc()
                            nc.vector.wait_ge(ss.sem, exp_done[c])
                            if c >= 2:
                                nc.vector.wait_ge(ts.sem, pvd[c - 2])
                            elif qb > 0:
                                nc.vector.wait_ge(ts.sem, prev_pv_last)
                            tt = None
                            for hh in range(2):
                                hg = 2 * c + hh
                                tt = nc.vector.tensor_tensor(
                                    P2[c % 2][:, hh * 1024: hh * 1024 + nv * 128],
                                    P1[c % 2][:, hh * 1024: hh * 1024 + nv * 128],
                                    M_all[:, hg * 1024: hg * 1024 + nv * 128], ALU.mult)
                            tt.then_inc(vs.sem, 1)
                            msk_done[c] = vs.inc()
                            if debug and s == 0 and qb == 3 and c == 0:
                                nc.sync.wait_ge(vs.sem, msk_done[c])
                                nc.sync.dma_start(dbg["d_P1"][:, :], P1[c % 2][:, :]).then_inc(qd.sem, 16)
                                nc.sync.dma_start(dbg["d_P2"][:, :], P2[0][:, :]).then_inc(qd.sem, 16)
                                DBGP = qd.inc(32)
                                nc.scalar.wait_ge(qd.sem, DBGP)
                                nc.vector.wait_ge(qd.sem, DBGP)
                                nc.tensor.wait_ge(qd.sem, DBGP)
                            if c >= 1:
                                pvd[c - 1] = emit_pv(c - 1)
                        pvd[7] = emit_pv(7)
                        prev_pv_last = pvd[7]
                        prev_msk_m1 = msk_done[7]
                        prev_msk_m2 = msk_done[6]
                        nc.vector.wait_ge(ts.sem, pvd[7])
                        for g, ng in ((0, 7), (1, 7), (2, 2)):
                            vw = psMulti[:, g * 512: g * 512 + ng * 65].rearrange(
                                "p (u r) -> p u r", r=65)
                            inst = nc.vector.reciprocal(rec16[:, g * 7: g * 7 + ng], vw[:, :, 64:65])
                        vedge(inst)
                        tt = None
                        for u in range(16):
                            tt = nc.vector.tensor_scalar_mul(
                                flat_qb[:, u * 64:(u + 1) * 64],
                                psMulti[:, pvcol(u): pvcol(u) + 64],
                                rec16[:, u:u + 1])
                        tt.then_inc(vs.sem, 1)
                        vs.inc()
                        if debug and s == 0 and qb == 3:
                            nc.sync.wait_ge(vs.sem, vs.v)
                            nc.sync.dma_start(dbg["d_flat"][:, :], flat_qb[:, :]).then_inc(qd.sem, 16)
                            DBGF = qd.inc(16)
                            nc.vector.wait_ge(qd.sem, DBGF)
                        fg_sb = q_qb[bi]
                        fgT = gate_one
                        nc.vector.wait_ge(s_gx.sem, GXLOAD)
                        nc.vector.wait_ge(vs.sem, vs.v)
                        nc.vector.tensor_tensor(fg_sb[:, :], flat_qb[:, :], gate_qb[bi][:, :],
                                                ALU.mult).then_inc(vs.sem, 1)
                        FG = vs.inc()
                        nc.tensor.wait_ge(vs.sem, FG)
                        mm = None
                        for ch in range(8):
                            mm = nc.tensor.transpose(psT2[:, ch * 128:(ch + 1) * 128],
                                                     fg_sb[:, ch * 128:(ch + 1) * 128], ident_sb[:, :])
                        mm.then_inc(ts.sem, 1)
                        FGT = ts.inc()
                        nc.vector.wait_ge(ts.sem, FGT)
                        nc.vector.tensor_copy(fgT[:, :], psT2[:, :]).then_inc(vs.sem, 1)
                        FGE = vs.inc()
                        nc.tensor.wait_ge(vs.sem, FGE)
                        nc.tensor.wait_ge(s_wg.sem, WO_L)
                        for kc in range(8):
                            nc.tensor.matmul(psMulti[:, 0:512], fgT[:, kc * 128:(kc + 1) * 128],
                                             WgWo_bf[:, kc * 1024: kc * 1024 + 512],
                                             start=(kc == 0), stop=False)
                            nc.tensor.matmul(psMulti[:, 512:1024], fgT[:, kc * 128:(kc + 1) * 128],
                                             WgWo_bf[:, kc * 1024 + 512: kc * 1024 + 1024],
                                             start=(kc == 0), stop=False)
                        nc.tensor.matmul(psMulti[:, 0:512], onescol_sb[:, :], bo_sb[:, 0:512],
                                         start=False, stop=False)
                        mm = nc.tensor.matmul(psMulti[:, 512:1024], onescol_sb[:, :], bo_sb[:, 512:1024],
                                              start=False, stop=True)
                        mm.then_inc(ts.sem, 1)
                        WOD = ts.inc()
                        nc.vector.wait_ge(ts.sem, WOD)
                        if part_guard[bi]:
                            nc.vector.wait_ge(s_pst.sem, part_guard[bi])
                        PTE = vflush(nc.vector.tensor_tensor(part_ev[bi][:, :], psMulti[:, 0:1024],
                                                              x_qb[bi][:, :], ALU.add))
                        prev_PTE = PTE
                        nc.sync.wait_ge(vs.sem, PTE)
                        nc.sync.dma_start(partial_d[g0:g0 + 128, :], part_ev[bi][:, :]).then_inc(s_pst.sem, 16)
                        part_guard[0] = part_guard[1] = s_pst.inc(16)
                        pf_guard[bi] = [(ts.sem, ts.v), (vs.sem, vs.v)]
                        gx_guard = [(ts.sem, ts.v), (vs.sem, vs.v)]
                    for c_ in [s_bld[0], s_bld[1], s_gx, s_pst]:
                        nc.sync.wait_ge(c_.sem, c_.v)
                    nc.gpsimd.wait_ge(s_wg.sem, s_wg.v)
                    nc.all_engine_barrier()

        w1buf_ctx = ExitStack()
        w1buf = w1buf_ctx.enter_context(nc.sbuf_tensor("w1buf", [128, 8 * 4096], BF16))
        nc.gpsimd.dma_start(
            w1buf[:, :].rearrange("p (kc f) -> p kc f", kc=8),
            w1[0].rearrange("(kc p) f -> kc p f", p=128).transpose([1, 0, 2]),
        ).then_inc(s_w1.sem, 16)
        s_w1.inc(16)
        W1_0 = s_w1.v
        with ExitStack() as pc:
            psb = lambda n, sh, d: pc.enter_context(nc.sbuf_tensor(n, sh, d))
            pps = lambda n, sh, d: pc.enter_context(nc.psum_tensor(n, sh, d))
            logits_fm = psb("logits_fm", [8, TC], F32)
            lt_tm = psb("lt_tm", [128, 256], F32)
            e8x = psb("e8x", [128, 256], F32)
            e8y = psb("e8y", [128, 256], F32)
            m8 = psb("m8", [128, 32], F32)
            s8 = psb("s8", [128, 32], F32)
            mask_tm = psb("mask_tm", [128, 256], BF16)
            mask_fm = psb("mask_fm", [8, TC], F32)
            csA = psb("csA", [8, TC], F32)
            csB = psb("csB", [8, TC], F32)
            qf = psb("qf", [8, TC], F32)
            qi = psb("qi", [8, TC], I32)
            rf = psb("rf", [8, TC], F32)
            posb = psb("posb", [8, TC], F32)
            dest_i = psb("dest_i", [128, 32], I32)
            psL2 = pps("psL2", [128, 8], F32)
            psM = pps("psM", [8, 128], F32)
            psD2 = pps("psD2", [128, 32], F32)

            nc.sync.dma_start(logits_fm[:, :], logits_d[:, :]).then_inc(s_lgl.sem, 16)
            LGF = s_lgl.inc(16)
            nc.tensor.wait_ge(qd.sem, CONSTS_ALL)
            nc.tensor.wait_ge(s_lgl.sem, LGF)
            for b in range(NBLK):
                nc.tensor.matmul(psL2[:, :], logits_fm[:, b * 128:(b + 1) * 128],
                                 i8_sb[:, :], start=True, stop=True).then_inc(ts.sem, 1)
                LT = ts.inc()
                nc.vector.wait_ge(ts.sem, LT)
                nc.vector.tensor_copy(lt_tm[:, b * 8:(b + 1) * 8], psL2[:, :]).then_inc(vs.sem, 1)
                vs.inc()
                nc.tensor.wait_ge(vs.sem, vs.v)
            nc.vector.wait_ge(vs.sem, vs.v)
            vedge(nc.vector.reduce_max(m8[:, :], lt_tm[:, :].rearrange("p (b e) -> p b e", e=8), axis=AX.X))
            tt = None
            for b in range(NBLK):
                tt = nc.vector.tensor_scalar(e8x[:, b * 8:(b + 1) * 8], lt_tm[:, b * 8:(b + 1) * 8],
                                             m8[:, b:b + 1], None, ALU.subtract)
            tt.then_inc(vs.sem, 1)
            SUBD = vs.inc()
            nc.scalar.wait_ge(vs.sem, SUBD)
            nc.scalar.activation(e8y[:, :], e8x[:, :], AF.Exp).then_inc(ss.sem, 1)
            EXPD = ss.inc()
            nc.vector.wait_ge(ss.sem, EXPD)
            vedge(nc.vector.reduce_sum(s8[:, :], e8y[:, :].rearrange("p (b e) -> p b e", e=8), axis=AX.X))
            nc.vector.reciprocal(pm_tm[:, :], s8[:, :])
            tt = None
            for b in range(NBLK):
                tt = nc.vector.tensor_scalar(mask_tm[:, b * 8:(b + 1) * 8], lt_tm[:, b * 8:(b + 1) * 8],
                                             m8[:, b:b + 1], None, ALU.is_ge)
            tt.then_inc(vs.sem, 1)
            MTM = vs.inc()
            nc.tensor.wait_ge(vs.sem, MTM)
            for b in range(NBLK):
                nc.tensor.matmul(psM[:, :], mask_tm[:, b * 8:(b + 1) * 8], ident_sb[:, :],
                                 start=True, stop=True).then_inc(ts.sem, 1)
                MF = ts.inc()
                nc.vector.wait_ge(ts.sem, MF)
                nc.vector.tensor_copy(mask_fm[:, b * 128:(b + 1) * 128], psM[:, :]).then_inc(vs.sem, 1)
                vs.inc()
                nc.tensor.wait_ge(vs.sem, vs.v)
            nc.vector.wait_ge(vs.sem, vs.v)
            vedge(nc.vector.tensor_copy(csA[:, :], mask_fm[:, :]))
            src, dst = csA, csB
            sh = 1
            while sh < TC:
                nc.vector.tensor_tensor(dst[:, sh:], src[:, sh:], src[:, 0:TC - sh], ALU.add)
                vedge(nc.vector.tensor_copy(dst[:, 0:sh], src[:, 0:sh]))
                src, dst = dst, src
                sh *= 2
            csum = src
            assert csum is csA, "cumsum parity: result must land in csA"
            vbuf = csB
            vedge(nc.vector.tensor_scalar_add(vbuf[:, :], csum[:, :], -1.0))
            pos2 = csA
            vedge(nc.vector.tensor_scalar(qf[:, :], vbuf[:, :], 63.5, 1.0 / 128.0, ALU.subtract, ALU.mult))
            vedge(nc.vector.tensor_copy(qi[:, :], qf[:, :]))
            vedge(nc.vector.tensor_copy(qf[:, :], qi[:, :]))
            vedge(nc.vector.scalar_tensor_tensor(rf[:, :], qf[:, :], -128.0, vbuf[:, :], ALU.mult, ALU.add))
            vedge(nc.vector.scalar_tensor_tensor(posb[:, :], rf[:, :], float(TOTC), qf[:, :], ALU.mult, ALU.add))
            vedge(nc.vector.tensor_scalar_add(pos2[:, :], posb[:, :], e8_sb[:, 0:1]))
            vedge(nc.vector.tensor_single_scalar(rf[:, :], vbuf[:, :], C - 0.5, ALU.is_ge))
            vedge(nc.vector.scalar_tensor_tensor(posb[:, :], rf[:, :], 1.0e7, pos2[:, :], ALU.mult, ALU.add))
            nc.vector.tensor_tensor(pos2[:, :], posb[:, :], mask_fm[:, :],
                                    ALU.mult).then_inc(vs.sem, 1)
            POSD = vs.inc()
            nc.tensor.wait_ge(vs.sem, POSD)
            mm = None
            for b in range(NBLK):
                mm = nc.tensor.matmul(psD2[:, b:b + 1], pos2[:, b * 128:(b + 1) * 128],
                                      ones8_sb[:, :], start=True, stop=True)
            mm.then_inc(ts.sem, 1)
            DSTT = ts.inc()
            nc.vector.wait_ge(ts.sem, DSTT)
            DSTI = vflush(nc.vector.tensor_copy(dest_i[:, :], psD2[:, :]))
            nc.gpsimd.wait_ge(vs.sem, DSTI)
            with nc.gpsimd.register("bc_c") as bc_c:
                nc.gpsimd.reg_mov(bc_c, E * C - 1)
                for b in range(NBLK):
                    nc.gpsimd.indirect_dma_start(
                        g_d[:, :], IOA(dest_i[:, b:b + 1], 0),
                        iota_sb[:, b:b + 1], None,
                        bounds_check=bc_c, oob_is_err=False,
                    ).then_inc(s_gsc.sem, 16)
                    s_gsc.inc(16)
            if debug:
                nc.sync.wait_ge(vs.sem, vs.v)
                nc.sync.wait_ge(s_gsc.sem, s_gsc.v)
                nc.sync.dma_start(dbg["d_mask"][:, :], mask_fm[:, :]).then_inc(qd.sem, 16)
                nc.sync.dma_start(dbg["d_pm"][:, :], pm_tm[:, :]).then_inc(qd.sem, 16)
                nc.sync.dma_start(dbg["d_dest"][:, :], dest_i[:, :]).then_inc(qd.sem, 16)
                nc.sync.dma_start(dbg["d_g"][:, :], g_d[:, :]).then_inc(qd.sem, 16)
                qd.inc(64)
            nc.gpsimd.wait_ge(s_gsc.sem, s_gsc.v)
            nc.sync.wait_ge(s_lgl.sem, s_lgl.v)
            nc.all_engine_barrier()

        with ExitStack() as pd_:
            psb = lambda n, sh, d: pd_.enter_context(nc.sbuf_tensor(n, sh, d))
            pps = lambda n, sh, d: pd_.enter_context(nc.psum_tensor(n, sh, d))
            g_sb = psb("g_sb", [128, TOTC], I32)
            w2buf = psb("w2buf", [128, 32 * 1024], BF16)
            xg_tm = psb("xg_tm", [128, RB * 1024], BF16)
            xg_fm = psb("xg_fm", [128, 8 * C], BF16)
            mh = psb("mh", [128, 32 * 512], BF16)
            y_sb = [psb(f"y_sb{i}", [128, 1024], BF16) for i in range(2)]
            yev_done = [0, 0]
            y_guard = [0, 0]
            b2r = psb("b2r", [1, 2048], F32)
            psT3 = [pps(f"psT3_{i}", [128, 1024], BF16) for i in range(2)]
            psH = [pps(f"psH{i}", [128, 512], F32) for i in range(2)]
            psY = [pps(f"psY{i}", [128, 1024], F32) for i in range(2)]

            nc.sync.dma_start(g_sb[:, :],
                              g_d[:, :].rearrange("(p c) o -> p (c o)", p=128)).then_inc(s_gld.sem, 16)
            GSB = s_gld.inc(16)
            nc.gpsimd.wait_ge(s_gld.sem, GSB)
            nc.tensor.wait_ge(s_gld.sem, GSB)
            nc.tensor.wait_ge(qd.sem, CONSTS_ALL)
            nc.scalar.wait_ge(qd.sem, CONSTS_ALL)
            nc.vector.memset(xg_tm[:, :], 0.0).then_inc(vs.sem, 1)
            XGZ = vs.inc()
            nc.gpsimd.wait_ge(vs.sem, XGZ)
            bc_d = pd_.enter_context(nc.gpsimd.register("bc_d"))
            nc.gpsimd.reg_mov(bc_d, TC - 1)

            ysc_guard = 0
            b2_guard = [0, 0]
            w1_ready = [0] * (E + 1)
            w2_ready = [0] * (E + 1)
            xg_ready = [0] * (E + 1)
            b2_ready = [0] * (E + 1)

            def load_w1(e):
                nc.gpsimd.dma_start(
                    w1buf[:, :].rearrange("p (kc f) -> p kc f", kc=8),
                    w1[e].rearrange("(kc p) f -> kc p f", p=128).transpose([1, 0, 2]),
                ).then_inc(s_w1.sem, 16)
                s_w1.inc(16)
                w1_ready[e] = s_w1.v

            def load_w2(e):
                for g4 in range(4):
                    nc.gpsimd.dma_start(
                        w2buf[:, g4 * 8192:(g4 + 1) * 8192].rearrange("p (fj f) -> p fj f", fj=8),
                        w2[e, g4 * 1024:(g4 + 1) * 1024, :].rearrange("(fj p) f -> fj p f", p=128).transpose([1, 0, 2]),
                    ).then_inc(s_w2.sem, 16)
                    s_w2.inc(16)
                w2_ready[e] = s_w2.v

            def load_xg(e):
                for rb in range(RB):
                    nc.gpsimd.indirect_dma_start(
                        xg_tm[:, rb * 1024:(rb + 1) * 1024], None,
                        xn_tm_d[:, :], IOA(g_sb[:, e * RB + rb: e * RB + rb + 1], 0),
                        bounds_check=bc_d, oob_is_err=False,
                    ).then_inc(s_xg.sem, 16)
                    s_xg.inc(16)
                xg_ready[e] = s_xg.v

            def load_b2(e):
                sl = e % 2
                if b2_guard[sl]:
                    nc.sync.wait_ge(ts.sem, b2_guard[sl])
                nc.sync.dma_start(b2r[:, sl * 1024:(sl + 1) * 1024],
                                  b2_rows[e:e + 1, :]).then_inc(s_b2[sl].sem, 16)
                b2_ready[e] = s_b2[sl].inc(16)

            w1_ready[0] = W1_0
            load_b2(0)
            load_xg(0)
            load_w2(0)
            for e in range(E):
                sl = e % 2
                nc.tensor.wait_ge(s_xg.sem, xg_ready[e])
                t3_done = [0, 0]
                for rb in range(RB):
                    ri = rb % 2
                    if t3_done[ri]:
                        nc.tensor.wait_ge(vs.sem, t3_done[ri])
                    mm = None
                    for ch in range(8):
                        mm = nc.tensor.transpose(psT3[ri][:, ch * 128:(ch + 1) * 128],
                                                 xg_tm[:, rb * 1024 + ch * 128: rb * 1024 + (ch + 1) * 128],
                                                 ident_sb[:, :])
                    mm.then_inc(ts.sem, 1)
                    XT = ts.inc()
                    nc.vector.wait_ge(ts.sem, XT)
                    out_ap = xg_fm[:, :].rearrange("p (c t) -> p c t", c=8)[:, :, rb * 128:(rb + 1) * 128]
                    in_ap = psT3[ri][:, :].rearrange("p (c t) -> p c t", c=8)
                    nc.vector.tensor_copy(out_ap, in_ap).then_inc(vs.sem, 1)
                    t3_done[ri] = vs.inc()
                XGF = vs.v
                TRD = ts.v
                if e + 1 < E:
                    nc.gpsimd.wait_ge(ts.sem, TRD)
                    load_xg(e + 1)
                    load_b2(e + 1)
                nc.tensor.wait_ge(vs.sem, XGF)
                nc.tensor.wait_ge(s_w1.sem, w1_ready[e])
                gelu_done = [0, 0]
                for csub in range(2):
                    c0, cw = CSP[csub]
                    nb4 = cw // 128
                    for fj in range(32):
                        pi = fj % 2
                        if gelu_done[pi]:
                            nc.tensor.wait_ge(ss.sem, gelu_done[pi])
                        mm = None
                        for kc in range(8):
                            mm = nc.tensor.matmul(
                                psH[pi][:, 0:cw],
                                w1buf[:, kc * 4096 + fj * 128: kc * 4096 + (fj + 1) * 128],
                                xg_fm[:, kc * C + c0: kc * C + c0 + cw],
                                start=(kc == 0), stop=(kc == 7))
                        mm.then_inc(ts.sem, 1)
                        HMM = ts.inc()
                        nc.scalar.wait_ge(ts.sem, HMM)
                        nc.scalar.activation(mh[:, fj * 512: fj * 512 + cw], psH[pi][:, 0:cw],
                                             AF.Gelu_apprx_tanh,
                                             bias=b1c_sb[:, e * 32 + fj: e * 32 + fj + 1]).then_inc(ss.sem, 1)
                        gelu_done[pi] = ss.inc()
                    if csub == 1 and e + 1 < E:
                        nc.gpsimd.wait_ge(ts.sem, ts.v)
                        load_w1(e + 1)
                    MHD = ss.v
                    nc.tensor.wait_ge(ss.sem, MHD)
                    if csub == 0:
                        nc.tensor.wait_ge(s_w2.sem, w2_ready[e])
                        nc.tensor.wait_ge(s_b2[sl].sem, b2_ready[e])
                    for sb4 in range(nb4):
                        yi = sb4 % 2
                        if yev_done[yi]:
                            nc.tensor.wait_ge(ss.sem, yev_done[yi])
                        for fj in range(32):
                            nc.tensor.matmul(psY[yi][:, 0:512],
                                             mh[:, fj * 512 + sb4 * 128: fj * 512 + (sb4 + 1) * 128],
                                             w2buf[:, fj * 1024: fj * 1024 + 512],
                                             start=(fj == 0), stop=False)
                            nc.tensor.matmul(psY[yi][:, 512:1024],
                                             mh[:, fj * 512 + sb4 * 128: fj * 512 + (sb4 + 1) * 128],
                                             w2buf[:, fj * 1024 + 512: fj * 1024 + 1024],
                                             start=(fj == 0), stop=False)
                        nc.tensor.matmul(psY[yi][:, 0:512], onescol_sb[:, :], b2r[:, sl * 1024: sl * 1024 + 512],
                                         start=False, stop=False)
                        mm = nc.tensor.matmul(psY[yi][:, 512:1024], onescol_sb[:, :],
                                              b2r[:, sl * 1024 + 512: sl * 1024 + 1024],
                                              start=False, stop=True)
                        mm.then_inc(ts.sem, 1)
                        YMM = ts.inc()
                        nc.scalar.wait_ge(ts.sem, YMM)
                        if y_guard[yi]:
                            nc.scalar.wait_ge(s_sc[yi].sem, y_guard[yi])
                        YEV = sflush(nc.scalar.activation(y_sb[yi][:, :], psY[yi][:, :], AF.Identity))
                        yev_done[yi] = YEV
                        nc.gpsimd.wait_ge(ss.sem, YEV)
                        nc.gpsimd.indirect_dma_start(
                            yffn_d[:, :],
                            IOA(g_sb[:, e * RB + csub * 4 + sb4: e * RB + csub * 4 + sb4 + 1], 0),
                            y_sb[yi][:, :], None,
                            bounds_check=bc_d, oob_is_err=False,
                        ).then_inc(s_sc[yi].sem, 16)
                        y_guard[yi] = s_sc[yi].inc(16)
                b2_guard[sl] = ts.v
                if e + 1 < E:
                    nc.gpsimd.wait_ge(ts.sem, ts.v)
                    load_w2(e + 1)
            for c_ in [s_w1, s_w2, s_xg, s_sc[0], s_sc[1]]:
                nc.gpsimd.wait_ge(c_.sem, c_.v)
            for c_ in [s_b2[0], s_b2[1], s_gld]:
                nc.sync.wait_ge(c_.sem, c_.v)
            nc.all_engine_barrier()

        w1buf_ctx.close()
        if debug:
            nc.sync.dma_start(dbg["d_partial"][:, :], partial_d[:, :]).then_inc(qd.sem, 16)
            nc.sync.dma_start(dbg["d_yffn"][:, :], yffn_d[:, :]).then_inc(qd.sem, 16)
            qd.inc(32)
            nc.sync.wait_ge(qd.sem, qd.v)
        with ExitStack() as pe:
            psb = lambda n, sh, d: pe.enter_context(nc.sbuf_tensor(n, sh, d))
            NB_E = 4
            part_b = [psb(f"part_b{i}", [128, 1024], BF16) for i in range(NB_E)]
            yf_b = [psb(f"yf_b{i}", [128, 1024], BF16) for i in range(NB_E)]
            t1 = [psb(f"t1_{i}", [128, 1024], F32) for i in range(2)]
            o_sb = [psb(f"o_sb{i}", [128, 1024], F32) for i in range(NB_E)]
            in_done = [0] * NBLK
            cons_done = [0] * NBLK
            out_done = [0] * NBLK

            def issue_load(b):
                bi = b % NB_E
                if b >= NB_E and cons_done[b - NB_E]:
                    nc.sync.wait_ge(vs.sem, cons_done[b - NB_E])
                nc.sync.dma_start(part_b[bi][:, :], partial_d[b * 128:(b + 1) * 128, :]).then_inc(s_e[bi].sem, 16)
                nc.sync.dma_start(yf_b[bi][:, :], yffn_d[b * 128:(b + 1) * 128, :]).then_inc(s_e[bi].sem, 16)
                in_done[b] = s_e[bi].inc(32)

            for b in range(min(NB_E, NBLK)):
                issue_load(b)
            for b in range(NBLK):
                bi = b % NB_E
                nc.vector.wait_ge(s_e[bi].sem, in_done[b])
                if b >= NB_E:
                    nc.vector.wait_ge(s_eo[bi].sem, out_done[b - NB_E])
                vedge(nc.vector.tensor_scalar_mul(t1[b % 2][:, :], yf_b[bi][:, :], pm_tm[:, b:b + 1]))
                cons_done[b] = vflush(nc.vector.tensor_tensor(o_sb[bi][:, :], t1[b % 2][:, :],
                                                              part_b[bi][:, :], ALU.add))
                nc.sync.wait_ge(vs.sem, cons_done[b])
                nc.sync.dma_start(out_d[b * 128:(b + 1) * 128, :], o_sb[bi][:, :]).then_inc(s_eo[bi].sem, 16)
                out_done[b] = s_eo[bi].inc(16)
                if b + NB_E < NBLK:
                    issue_load(b + NB_E)
            for c_ in s_eo + s_e:
                nc.sync.wait_ge(c_.sem, c_.v)
            nc.all_engine_barrier()

    return nc


def _host_consts():
    ident = np.eye(128, dtype=bfdt)
    i8 = np.eye(8, dtype=np.float32)
    ones8 = np.ones((8, 1), dtype=np.float32)
    onescol = np.ones((1, 128), dtype=np.float32)
    iota_tm = (np.arange(128)[:, None] + 128 * np.arange(32)[None, :]).astype(np.int32)
    idxM = np.zeros((128, 128), dtype=np.int32)
    for h in range(16):
        for di, d in enumerate(DSET):
            idxM[:, h * 8 + di] = h * WLEN + 127 + 128 * d - np.arange(128)
    e8 = (np.arange(8)[:, None] * RB).astype(np.float32)
    return ident, i8, ones8, onescol, iota_tm, idxM, e8


_prog_cache = {}


def kernel(L0_attn_x, L0_attn_Wqkv, L0_attn_bqkv, L0_attn_Wout, L0_attn_bout,
           L0_attn_Wgate, L0_attn_bgate, L0_attn_pos_bias,
           L0_ffn_switch_w, L0_ffn_switch_b, L0_ffn_w1, L0_ffn_b1,
           L0_ffn_w2, L0_ffn_b2):
    x = np.ascontiguousarray(np.asarray(L0_attn_x, dtype=np.float32))
    Wqkv = np.ascontiguousarray(np.asarray(L0_attn_Wqkv, dtype=np.float32))
    bqkv = np.asarray(L0_attn_bqkv, dtype=np.float32)
    Wout = np.ascontiguousarray(np.asarray(L0_attn_Wout, dtype=np.float32))
    bout = np.asarray(L0_attn_bout, dtype=np.float32)
    Wgate = np.ascontiguousarray(np.asarray(L0_attn_Wgate, dtype=np.float32))
    bgate = np.asarray(L0_attn_bgate, dtype=np.float32)
    pb = np.asarray(L0_attn_pos_bias, dtype=np.float32)
    sww = np.ascontiguousarray(np.asarray(L0_ffn_switch_w, dtype=np.float32))
    swb = np.asarray(L0_ffn_switch_b, dtype=np.float32)
    w1 = np.ascontiguousarray(np.asarray(L0_ffn_w1, dtype=np.float32))
    b1 = np.asarray(L0_ffn_b1, dtype=np.float32)
    w2 = np.ascontiguousarray(np.asarray(L0_ffn_w2, dtype=np.float32))
    b2 = np.ascontiguousarray(np.asarray(L0_ffn_b2, dtype=np.float32))

    bqkv_cols = np.ascontiguousarray(bqkv[:2048].reshape(16, 128).T)
    bv_row = np.ascontiguousarray(bqkv[2048:3072].reshape(1, D))
    bg_row = np.ascontiguousarray(bgate.reshape(1, D))
    bo_row = np.ascontiguousarray(bout.reshape(1, D))
    swb_col = np.ascontiguousarray(swb.reshape(8, 1))
    b1_cols = np.ascontiguousarray(b1.reshape(E * 32, 128).T)
    pbT = np.ascontiguousarray(pb.T)

    ident, i8, ones8, onescol, iota_tm, idxM, e8 = _host_consts()

    if "nc" not in _prog_cache:
        _prog_cache["nc"] = build_program()
    nc = _prog_cache["nc"]

    shared = dict(Wqkv=Wqkv, Wgate=Wgate, Wout=Wout, sww=sww, w1=w1, w2=w2,
                  bqkv_cols=bqkv_cols, bv_row=bv_row, bg_row=bg_row, bo_row=bo_row,
                  swb_col=swb_col, b1_cols=b1_cols, b2_rows=b2, pbT=pbT,
                  ident_bf=ident, i8c=i8, ones8=ones8, onescol=onescol,
                  iota_tm_c=iota_tm, idxM=idxM, e8col=e8)
    in_maps = []
    for core in range(NCORES):
        shard = x[core * SEQ_PER_CORE:(core + 1) * SEQ_PER_CORE].reshape(TC, D)
        m = dict(shared)
        m["x"] = np.ascontiguousarray(shard)
        in_maps.append(m)

    res = run_bass_kernel_spmd(nc, in_maps, core_ids=list(range(NCORES)))
    outs = [np.asarray(r["out"], dtype=np.float32).reshape(SEQ_PER_CORE, N, D)
            for r in res.results]
    return np.concatenate(outs, axis=0)
